# version 45 (final)
# baseline (speedup 1.0000x reference)
import numpy as np
import ml_dtypes
import concourse.bass as bass
import concourse.mybir as mybir
from concourse.bass_utils import run_bass_kernel_spmd
from concourse.aot_env import allow_ici_all_to_all
from contextlib import ExitStack

F32 = mybir.dt.float32
BF16 = mybir.dt.bfloat16
I32 = mybir.dt.int32
AF = mybir.ActivationFunctionType
ALU = mybir.AluOpType

R = 8
B, SEQ, D = 2, 8192, 2048
H, HD = 16, 128
T = 2048
FC = 16
INTER = 5504
OC = 43
LUT = 16
EXP_SHIFT = 8.0
INV_D = 1.0 / 2048.0


class Ctx:
    def __init__(self, nc, stack):
        self.nc = nc
        self.stack = stack
        self.c = {}
        self.sem = {}
        self.last_wait = {}
        for name in ("qS", "qG", "sCC", "sPE", "sACT", "sDVE"):
            self.sem[name] = stack.enter_context(nc.semaphore(name))
            self.c[name] = 0

    def _sem(self, name):
        if name not in self.sem:
            self.sem[name] = self.stack.enter_context(self.nc.semaphore(name))
            self.c[name] = 0
        return self.sem[name]

    def lds(self, sname, dst, src):
        s = self._sem(sname)
        instr = self.nc.sync.dma_start(dst, src)
        instr.then_inc(s, 16)
        self.c[sname] += 16
        return self.c[sname]

    def lda(self, sname, dst, src):
        s = self._sem(sname)
        instr = self.nc.scalar.dma_start(dst, src)
        instr.then_inc(s, 16)
        self.c[sname] += 16
        return self.c[sname]

    def inc(self, instr, name, n=1):
        instr.then_inc(self.sem[name], n)
        self.c[name] += n
        return self.c[name]

    def dinc(self, eng, name, n=1):
        return self.inc(eng.drain(), name, n)

    def ld(self, dst, src):
        return self.inc(self.nc.sync.dma_start(dst, src), "qS", 16)

    def st(self, dst, src):
        return self.inc(self.nc.gpsimd.dma_start(dst, src), "qG", 16)

    def wt(self, eng, ename, sname, v):
        if v is None or v <= 0:
            return
        self._sem(sname)
        if sname in ("qS", "qG"):
            v = self.c[sname]
        key = (ename, sname)
        if self.last_wait.get(key, 0) >= v:
            return
        eng.wait_ge(self.sem[sname], v)
        self.last_wait[key] = v

    def fence(self):
        nc = self.nc
        engs = [(nc.sync, "SY"), (nc.gpsimd, "GP"), (nc.tensor, "PE"),
                (nc.scalar, "ACT"), (nc.vector, "DVE")]
        own = {"PE": "sPE", "ACT": "sACT", "DVE": "sDVE"}
        snap = dict(self.c)
        for eng, en in engs:
            for sname in ("qS", "qG", "sPE", "sACT", "sDVE"):
                if own.get(en) == sname:
                    continue
                self.wt(eng, en, sname, snap[sname])


def key_loc(kb):
    kc = kb // 2
    jk = min(kc, 7 - kc)
    base = (0 if kc <= 3 else 1024) + (kb % 2) * 512
    return jk, base


def build(debug=False):
    nc = bass.Bass("TRN2", target_bir_lowering=False)
    dt_ = nc.dram_tensor
    xT = dt_("xT", [D, T], F32, kind="ExternalInput").ap()
    qkvwT = dt_("qkvwT", [D, 3 * D], F32, kind="ExternalInput").ap()
    qkvbT = dt_("qkvbT", [128, 48], F32, kind="ExternalInput").ap()
    vbiasI = dt_("vbias", [1, D], F32, kind="ExternalInput").ap()
    outwT = dt_("outwT", [D, D], F32, kind="ExternalInput").ap()
    outbT = dt_("outbT", [128, 16], F32, kind="ExternalInput").ap()
    lutI = dt_("lut", [1, LUT], F32, kind="ExternalInput").ap()
    gidx = dt_("gidx", [256, INTER], I32, kind="ExternalInput").ap()
    uidx = dt_("uidx", [256, INTER], I32, kind="ExternalInput").ap()
    didx = dt_("didx", [INTER, 256], I32, kind="ExternalInput").ap()
    embqI = dt_("embq", [128, T], F32, kind="ExternalInput").ap()
    embkI = dt_("embk", [128, T], F32, kind="ExternalInput").ap()
    masksI = dt_("masks", [128, 4 * 512], BF16, kind="ExternalInput").ap()
    outD = dt_("out", [D, T], F32, kind="ExternalOutput").ap()

    wg_sh = dt_("wg_sh", [2 * OC * 128, 128], BF16)
    wu_sh = dt_("wu_sh", [2 * OC * 128, 128], BF16)
    wd_sh = dt_("wd_sh", [2 * OC * 128, 128], BF16)
    wgD = dt_("wg", [16 * OC * 128, 128], BF16)
    wuD = dt_("wu", [16 * OC * 128, 128], BF16)
    wdD = dt_("wd", [16 * OC * 128, 128], BF16)
    qk_in = [dt_(f"qk_in{i}", [R * 512, 1024], BF16) for i in range(2)]
    qk_att = [dt_(f"qk_att{i}", [R * 512, 1024], BF16) for i in range(2)]
    v_in = [dt_(f"v_in{i}", [R * 1024, 256], BF16) for i in range(2)]
    v_att = [dt_(f"v_att{i}", [R * 1024, 256], BF16) for i in range(2)]
    o_in = [dt_(f"o_in{i}", [R * 128, T], BF16) for i in range(2)]
    o_att = [dt_(f"o_att{i}", [R * 128, T], BF16) for i in range(2)]
    x1D = dt_("x1", [D, T], F32)
    x2D = dt_("x2", [D, T], F32)

    with ExitStack() as stack:
        e = stack.enter_context
        ctx = Ctx(nc, stack)
        cnt = ctx.c
        wt, ld, st = ctx.wt, ctx.ld, ctx.st
        syn, gp, pe, act, dve = nc.sync, nc.gpsimd, nc.tensor, nc.scalar, nc.vector

        c_masks = e(nc.sbuf_tensor("c_masks", [128, 4 * 512], BF16))
        c_qkvbT = e(nc.sbuf_tensor("c_qkvbT", [128, 48], F32))
        c_outbT = e(nc.sbuf_tensor("c_outbT", [128, 16], F32))
        c_lut = e(nc.sbuf_tensor("c_lut", [1, LUT], F32))
        c_vbias = e(nc.sbuf_tensor("c_vbias", [1, D], F32))
        ones1 = e(nc.sbuf_tensor("ones1", [1, 128], F32))
        onesb = e(nc.sbuf_tensor("onesb", [128, 1], BF16))
        lutbc = e(nc.sbuf_tensor("lutbc", [128, LUT], F32))
        lutbch = e(nc.sbuf_tensor("lutbch", [128, LUT], F32))
        vbias_bc = e(nc.sbuf_tensor("vbias_bc", [128, D], BF16))
        epsc = e(nc.sbuf_tensor("epsc", [1, 1], F32))
        shiftc = e(nc.sbuf_tensor("shiftc", [128, 1], F32))

        ctx.lds("cst", c_masks[:], masksI)
        ctx.lds("cst", c_qkvbT[:], qkvbT)
        ctx.lds("cst", c_outbT[:], outbT)
        ctx.lds("cst", c_lut[:], lutI)
        consts_loaded = ctx.lds("cst", c_vbias[:], vbiasI)

        dve.memset(ones1[:], 1.0)
        dve.memset(onesb[:], 1.0)
        dve.memset(epsc[:], 1e-5)
        with nc.psum_tensor("ps_lbc", [128, LUT], F32) as ps_lbc, \
             nc.psum_tensor("ps_vb", [128, D], F32) as ps_vb:
            dve.memset(shiftc[:], -EXP_SHIFT)
            msets = ctx.dinc(dve, "sDVE")
            wt(pe, "PE", "cst", consts_loaded)
            wt(pe, "PE", "sDVE", msets)
            pe.matmul(ps_lbc[:], ones1[:], c_lut[:], start=True, stop=True)
            for i in range(4):
                pe.matmul(ps_vb[:, i * 512:(i + 1) * 512], ones1[:],
                          c_vbias[:, i * 512:(i + 1) * 512], start=True, stop=True)
            v0 = ctx.dinc(pe, "sPE")
            wt(dve, "DVE", "sPE", v0)
            dve.tensor_copy(lutbc[:], ps_lbc[:])
            dve.tensor_scalar(lutbch[:], ps_lbc[:], 0.5, None, ALU.mult)
            dve.tensor_copy(vbias_bc[:], ps_vb[:])
            consts_done = ctx.dinc(dve, "sDVE")

        with nc.sbuf_tensor("m_idx", [128, 2 * INTER], I32) as m_idx, \
             nc.sbuf_tensor("m_idxb", [128, 2 * INTER], BF16) as m_idxb, \
             nc.sbuf_tensor("m_acc", [128, 2 * INTER], BF16) as m_acc, \
             nc.sbuf_tensor("m_term", [128, 2 * INTER], BF16) as m_term:

            mat_prev = {"vd": 0, "stores": 0}

            def materialize(kind, idx_ap, sh, full, lcol):
                wt(syn, "SY", "sDVE", mat_prev["vd"])
                if kind == "g":
                    vl = ctx.lds("mix", m_idx[:].rearrange("p (l o) -> p l o", l=2),
                                 idx_ap.rearrange("(l p) o -> p l o", l=2, p=128))
                else:
                    vl = ctx.lds("mix", m_idx[:].rearrange("p (i m) -> p i m", i=OC),
                                 idx_ap.rearrange("(i p) m -> p i m", i=OC, p=128))
                wt(dve, "DVE", "mix", vl)
                wt(dve, "DVE", "sDVE", consts_done)
                wt(dve, "DVE", "qG", mat_prev["stores"])
                dve.tensor_copy(m_idxb[:], m_idx[:])
                for v in range(LUT):
                    dve.tensor_scalar(m_term[:], m_idxb[:], float(v),
                                      lcol[:, v:v + 1], ALU.is_equal, ALU.mult)
                    if v == 0:
                        dve.tensor_copy(m_acc[:], m_term[:])
                    else:
                        dve.tensor_tensor(m_acc[:], m_acc[:], m_term[:], ALU.add)
                vd = ctx.dinc(dve, "sDVE")
                mat_prev["vd"] = vd
                wt(gp, "GP", "sDVE", vd)
                shr = sh.ap().rearrange("(l o p) n -> l p o n", l=2, o=OC, p=128)
                if kind == "g":
                    srcr = m_acc[:].rearrange("p (l o n) -> p l o n", l=2, o=OC, n=128)
                    srcs = [srcr[:, 0], srcr[:, 1]]
                else:
                    srcr = m_acc[:].rearrange("p (i l n) -> p i l n", i=OC, l=2, n=128)
                    srcs = [srcr[:, :, 0], srcr[:, :, 1]]
                for blk in range(2):
                    st(shr[blk], srcs[blk])
                mat_prev["stores"] = cnt["qG"]
                wt(gp, "GP", "qG", cnt["qG"])
                return ctx.inc(
                    gp.collective_compute("AllGather", ALU.bypass,
                                          replica_groups=[list(range(R))],
                                          ins=[sh.ap().opt()], outs=[full.ap().opt()]),
                    "sCC")

            cc_g = materialize("g", gidx, wg_sh, wgD, lutbc)
            cc_u = materialize("g", uidx, wu_sh, wuD, lutbc)
            cc_d = materialize("d", didx, wd_sh, wdD, lutbch)

        ctx.fence()


        def emit_ln(src_r, xn, tau, tag, dep_qG=0):
            v_norm = 0
            with ExitStack() as lst:
                le = lst.enter_context
                xb = le(nc.sbuf_tensor(f"lx_{tag}", [128, FC * 512], BF16))
                xl = [le(nc.sbuf_tensor(f"ll{i}_{tag}", [128, 512], F32)) for i in range(4)]
                sq2 = [le(nc.sbuf_tensor(f"lq{i}_{tag}", [128, 512], BF16)) for i in range(2)]
                sts = le(nc.sbuf_tensor(f"ls_{tag}", [1, 512 * 6], F32))
                bcb = le(nc.sbuf_tensor(f"lb_{tag}", [128, 1024], BF16))
                mean, ex2 = sts[:, 0:512], sts[:, 512:1024]
                var, lnv = sts[:, 1024:1536], sts[:, 1536:2048]
                rr, mr = sts[:, 2048:2560], sts[:, 2560:3072]
                prev_pe = 0
                prev_norm = 0
                prev_cv_last = 0
                g_norm = [0, 0]
                for g in range(2):
                    t0 = tau * 1024 + g * 512
                    with nc.psum_tensor(f"lps_{tag}{g}", [1, 512], F32) as ps_su, \
                         nc.psum_tensor(f"lpq_{tag}{g}", [1, 512], F32) as ps_sq, \
                         nc.psum_tensor(f"lpb_{tag}{g}", [128, 1024], F32) as ps_b:
                        cvs, sqv, smm = {}, {}, {}
                        wt(syn, "SY", "qG", dep_qG)
                        for kc in range(FC):
                            if kc >= 4:
                                wt(syn, "SY", "sDVE", cvs[kc - 4])
                            else:
                                wt(syn, "SY", "sDVE", prev_cv_last)
                            vl = ctx.lds(f"xl{kc % 4}", xl[kc % 4][:],
                                         src_r[kc][:, t0:t0 + 512])
                            wt(dve, "DVE", f"xl{kc % 4}", vl)
                            if kc == 0:
                                wt(dve, "DVE", "sPE", prev_pe)
                            dve.tensor_copy(xb[:, kc * 512:(kc + 1) * 512], xl[kc % 4][:])
                            cvs[kc] = ctx.dinc(dve, "sDVE")
                            wt(act, "ACT", "sDVE", cvs[kc])
                            if kc == 0:
                                wt(act, "ACT", "sPE", prev_pe)
                            if kc >= 2:
                                wt(act, "ACT", "sPE", smm[kc - 2])
                            act.square(sq2[kc % 2][:],
                                       xb[:, kc * 512:(kc + 1) * 512])
                            sqv[kc] = ctx.dinc(act, "sACT")
                            wt(pe, "PE", "sACT", sqv[kc])
                            pe.matmul(ps_su[:], onesb[:], xb[:, kc * 512:(kc + 1) * 512],
                                      start=(kc == 0), stop=(kc == FC - 1))
                            mmq = pe.matmul(ps_sq[:], onesb[:], sq2[kc % 2][:],
                                            start=(kc == 0), stop=(kc == FC - 1))
                            smm[kc] = ctx.inc(mmq, "sPE")
                        prev_cv_last = cvs[FC - 1]
                        prev_pe = ctx.dinc(pe, "sPE")
                        wt(dve, "DVE", "sPE", prev_pe)
                        dve.tensor_scalar(mean, ps_su[:], INV_D, None, ALU.mult)
                        dve.tensor_scalar(ex2, ps_sq[:], INV_D, None, ALU.mult)
                        dve.tensor_tensor(var, mean, mean, ALU.mult)
                        dve.tensor_tensor(var, ex2, var, ALU.subtract)
                        v_var = ctx.dinc(dve, "sDVE")
                        wt(act, "ACT", "sDVE", v_var)
                        act.activation(lnv, var, AF.Ln, bias=epsc[:], scale=1.0)
                        act.activation(rr, lnv, AF.Exp, scale=-0.5)
                        v_r = ctx.dinc(act, "sACT")
                        wt(dve, "DVE", "sACT", v_r)
                        dve.tensor_tensor(mr, mean, rr, ALU.mult)
                        v_mr = ctx.dinc(dve, "sDVE")
                        wt(pe, "PE", "sDVE", v_mr)
                        wt(pe, "PE", "sDVE", prev_norm)
                        pe.matmul(ps_b[:, 0:512], ones1[:], rr, start=True, stop=True)
                        pe.matmul(ps_b[:, 512:1024], ones1[:], mr, start=True, stop=True)
                        v_bc = ctx.dinc(pe, "sPE")
                        wt(dve, "DVE", "sPE", v_bc)
                        dve.tensor_copy(bcb[:], ps_b[:])
                        for kc in range(FC):
                            col = kc * 1024 + g * 512
                            dve.tensor_tensor(xn[:, col:col + 512],
                                              xb[:, kc * 512:(kc + 1) * 512],
                                              bcb[:, 0:512], ALU.mult)
                            dve.tensor_tensor(xn[:, col:col + 512],
                                              xn[:, col:col + 512],
                                              bcb[:, 512:1024], ALU.subtract)
                        prev_norm = v_norm = ctx.dinc(dve, "sDVE")
                        g_norm[g] = v_norm
            return g_norm

        wgR = wgD.ap().rearrange("(kc o p) n -> o p kc n", kc=16, o=OC, p=128)
        wuR = wuD.ap().rearrange("(kc o p) n -> o p kc n", kc=16, o=OC, p=128)
        wdR = wdD.ap().rearrange("(mc i p) n -> mc p i n", mc=16, i=OC, p=128)
        xTr = xT.rearrange("(kc p) t -> kc p t", kc=FC, p=128)
        x1r = x1D.ap().rearrange("(kc p) t -> kc p t", kc=FC, p=128)
        x2r = x2D.ap().rearrange("(kc p) t -> kc p t", kc=FC, p=128)
        outr = outD.rearrange("(kc p) t -> kc p t", kc=FC, p=128)
        qwfR = qkvwT.rearrange("(kc p) c -> p kc c", kc=FC, p=128)
        owfr = outwT.rearrange("(kc p) c -> kc p c", kc=FC, p=128)

        def emit_ffn(src_r, res_r, dst_r, tag, cc_gu, cc_dn, dep_qG=0):
            for tau in range(2):
                ts = tau * 1024
                with ExitStack() as fst:
                    fe = fst.enter_context
                    xn = fe(nc.sbuf_tensor(f"xn_{tag}{tau}", [128, FC * 1024], BF16))
                    wbg = [fe(nc.sbuf_tensor(f"wbg{i}_{tag}{tau}", [128, 2048], BF16)) for i in range(3)]
                    wbu = [fe(nc.sbuf_tensor(f"wbu{i}_{tag}{tau}", [128, 2048], BF16)) for i in range(3)]
                    hsil = [fe(nc.sbuf_tensor(f"hs{i}_{tag}{tau}", [128, 1024], BF16)) for i in range(2)]
                    hbuf = fe(nc.sbuf_tensor(f"h_{tag}{tau}", [128, OC * 1024], BF16))
                    v_xng = emit_ln(src_r, xn, tau, f"{tag}{tau}", dep_qG=dep_qG)
                    v_xn = v_xng[1]
                    if True:
                        pe_done, silu_done, hm_done = {}, {}, {}
                        with nc.psum_tensor(f"pg0_{tag}{tau}", [128, 1024], F32) as pg0, \
                             nc.psum_tensor(f"pg1_{tag}{tau}", [128, 1024], F32) as pg1, \
                             nc.psum_tensor(f"pu0_{tag}{tau}", [128, 1024], F32) as pu0, \
                             nc.psum_tensor(f"pu1_{tag}{tau}", [128, 1024], F32) as pu1:
                            pg, pu = [pg0, pg1], [pu0, pu1]
                            for oc in range(OC):
                                if oc >= 3:
                                    wt(syn, "SY", "sPE", pe_done[oc - 3])
                                else:
                                    wt(syn, "SY", "sCC", cc_gu)
                                ctx.lds(f"gu{oc % 3}",
                                        wbg[oc % 3][:].rearrange("p (kc n) -> p kc n", kc=16),
                                        wgR[oc])
                                vlu = ctx.lds(f"gu{oc % 3}",
                                              wbu[oc % 3][:].rearrange("p (kc n) -> p kc n", kc=16),
                                              wuR[oc])
                                wt(pe, "PE", f"gu{oc % 3}", vlu)
                                if oc >= 2:
                                    wt(pe, "PE", "sACT", silu_done[oc - 2])
                                    wt(pe, "PE", "sDVE", hm_done[oc - 2])
                                for hf in range(2):
                                    wt(pe, "PE", "sDVE", v_xng[hf])
                                    for buf, ps in ((wbg, pg), (wbu, pu)):
                                        for kc in range(FC):
                                            pe.matmul(
                                                ps[oc % 2][:, hf * 512:(hf + 1) * 512],
                                                buf[oc % 3][:, kc * 128:(kc + 1) * 128],
                                                xn[:, kc * 1024 + hf * 512:
                                                   kc * 1024 + (hf + 1) * 512],
                                                start=(kc == 0), stop=(kc == FC - 1))
                                pe_done[oc] = ctx.dinc(pe, "sPE")
                                wt(act, "ACT", "sPE", pe_done[oc])
                                if oc >= 2:
                                    wt(act, "ACT", "sDVE", hm_done[oc - 2])
                                act.activation(hsil[oc % 2][:], pg[oc % 2][:], AF.Silu)
                                silu_done[oc] = ctx.dinc(act, "sACT")
                                wt(dve, "DVE", "sACT", silu_done[oc])
                                dve.tensor_tensor(hbuf[:, oc * 1024:(oc + 1) * 1024],
                                                  pu[oc % 2][:], hsil[oc % 2][:], ALU.mult)
                                hm_done[oc] = ctx.dinc(dve, "sDVE")
                        with nc.sbuf_tensor(f"wdh0_{tag}{tau}", [128, 22 * 128], BF16) as wdh0, \
                             nc.sbuf_tensor(f"wdh1_{tag}{tau}", [128, 22 * 128], BF16) as wdh1, \
                             nc.sbuf_tensor(f"wdh2_{tag}{tau}", [128, 22 * 128], BF16) as wdh2, \
                             nc.sbuf_tensor(f"xr0_{tag}{tau}", [128, 1024], F32) as xr0, \
                             nc.sbuf_tensor(f"xr1_{tag}{tau}", [128, 1024], F32) as xr1, \
                             nc.sbuf_tensor(f"so0_{tag}{tau}", [128, 1024], F32) as so0, \
                             nc.sbuf_tensor(f"so1_{tag}{tau}", [128, 1024], F32) as so1, \
                             nc.psum_tensor(f"pd0_{tag}{tau}", [128, 1024], F32) as pd0, \
                             nc.psum_tensor(f"pd1_{tag}{tau}", [128, 1024], F32) as pd1:
                            wdh, xres, sx1 = [wdh0, wdh1, wdh2], [xr0, xr1], [so0, so1]
                            pd = [pd0, pd1]
                            halves = [(0, 22), (22, 43)]
                            dstt, stst, hload = {}, {}, {}
                            for mc in range(16):
                                for hi, (i0, i1) in enumerate(halves):
                                    hidx = mc * 2 + hi
                                    if hidx >= 3:
                                        wt(syn, "SY", "sPE", pe_done[("d", hidx - 3)])
                                    else:
                                        wt(syn, "SY", "sCC", cc_dn)
                                    hload[hidx] = ctx.lds(
                                        f"dh{hidx % 3}",
                                        wdh[hidx % 3][:][:, 0:(i1 - i0) * 128].rearrange(
                                            "p (i n) -> p i n", n=128),
                                        wdR[mc][:, i0:i1, :])
                                    wt(pe, "PE", f"dh{hidx % 3}", hload[hidx])
                                    if mc == 0 and hi == 0:
                                        wt(pe, "PE", "sDVE", hm_done[OC - 1])
                                    if mc >= 2 and hi == 0:
                                        wt(pe, "PE", "sDVE", dstt[mc - 2])
                                    for i in range(i0, i1):
                                        for hf in range(2):
                                            pe.matmul(
                                                pd[mc % 2][:, hf * 512:(hf + 1) * 512],
                                                wdh[hidx % 3][:][:, (i - i0) * 128:(i - i0 + 1) * 128],
                                                hbuf[:, i * 1024 + hf * 512:
                                                     i * 1024 + (hf + 1) * 512],
                                                start=(i == 0), stop=(i == OC - 1))
                                    pe_done[("d", hidx)] = ctx.dinc(pe, "sPE")
                                if mc >= 2:
                                    wt(syn, "SY", "sDVE", dstt[mc - 2])
                                vxr = ctx.lds(f"xr{mc % 2}", xres[mc % 2][:],
                                              res_r[mc][:, ts:ts + 1024])
                                wt(dve, "DVE", "sPE", pe_done[("d", mc * 2 + 1)])
                                wt(dve, "DVE", f"xr{mc % 2}", vxr)
                                if mc >= 2:
                                    wt(dve, "DVE", "qG", stst[mc - 2])
                                dve.scalar_tensor_tensor(sx1[mc % 2][:], pd[mc % 2][:], 1.0,
                                                         xres[mc % 2][:], ALU.mult, ALU.add)
                                dstt[mc] = ctx.dinc(dve, "sDVE")
                                wt(gp, "GP", "sDVE", dstt[mc])
                                stst[mc] = st(dst_r[mc][:, ts:ts + 1024], sx1[mc % 2][:])
                    ctx.fence()
            return cnt["qG"]

        x1_qG = emit_ffn(xTr, xTr, x1r, "f1", cc_gu=cc_u, cc_dn=cc_d, dep_qG=0)

        for tau in range(2):
            ts = tau * 1024
            with ExitStack() as pst2:
                p2e = pst2.enter_context
                xn2 = p2e(nc.sbuf_tensor(f"xn2_{tau}", [128, FC * 1024], BF16))
                wqk = [p2e(nc.sbuf_tensor(f"wqk{i}_{tau}", [128, 2048], BF16)) for i in range(3)]
                wqf = [p2e(nc.sbuf_tensor(f"wqf{i}_{tau}", [128, 2048], F32)) for i in range(3)]
                sqk = [p2e(nc.sbuf_tensor(f"sqk{i}_{tau}", [128, 1024], BF16)) for i in range(2)]
                embB = [p2e(nc.sbuf_tensor(f"emb{i}_{tau}", [128, 1024], F32)) for i in range(2)]
                v_xn2g = emit_ln(x1r, xn2, tau, f"q{tau}", dep_qG=x1_qG)
                v_xn2 = v_xn2g[1]
                if True:
                    qcv = {}

                    def qk_conv(j, qk_pe):
                        if j >= 32:
                            return
                        if j >= 3:
                            wt(syn, "SY", "sDVE", qcv[j - 3])
                        vlf = ctx.lds(f"qwf{j % 3}",
                                      wqf[j % 3][:].rearrange("p (kc n) -> p kc n", kc=16),
                                      qwfR[:, :, j * 128:(j + 1) * 128])
                        wt(dve, "DVE", f"qwf{j % 3}", vlf)
                        if j >= 3:
                            wt(dve, "DVE", "sPE", qk_pe.get(j - 3, 0))
                        dve.tensor_copy(wqk[j % 3][:], wqf[j % 3][:])
                        qcv[j] = ctx.dinc(dve, "sDVE")
                    v_eq = ctx.lds("emb", embB[0][:], embqI[:, ts:ts + 1024])
                    v_ek = ctx.lds("emb", embB[1][:], embkI[:, ts:ts + 1024])
                    with ExitStack() as pst3:
                        pqk = [pst3.enter_context(nc.psum_tensor(f"pqk{i}_{tau}", [128, 1024], F32)) for i in range(2)]
                        qk_pe, qk_dve, qk_st = {}, {}, {}
                        qk_conv(0, qk_pe)
                        qk_conv(1, qk_pe)
                        qk_conv(2, qk_pe)
                        for oc in range(32):
                            wt(pe, "PE", "sDVE", qcv[oc])
                            if oc >= 2:
                                wt(pe, "PE", "sDVE", qk_dve[oc - 2])
                            for hf in range(2):
                                wt(pe, "PE", "sDVE", v_xn2g[hf])
                                for kc in range(FC):
                                    pe.matmul(
                                        pqk[oc % 2][:, hf * 512:(hf + 1) * 512],
                                        wqk[oc % 3][:][:, kc * 128:(kc + 1) * 128],
                                        xn2[:, kc * 1024 + hf * 512: kc * 1024 + (hf + 1) * 512],
                                        start=(kc == 0), stop=(kc == FC - 1))
                            qk_pe[oc] = ctx.dinc(pe, "sPE")
                            qk_conv(oc + 3, qk_pe)
                            wt(dve, "DVE", "sPE", qk_pe[oc])
                            wt(dve, "DVE", "emb", max(v_eq, v_ek))
                            if oc >= 2:
                                wt(dve, "DVE", "qG", qk_st[oc - 2])
                            embt = embB[0][:] if oc < 16 else embB[1][:]
                            dve.scalar_tensor_tensor(sqk[oc % 2][:], pqk[oc % 2][:],
                                                     c_qkvbT[:, oc:oc + 1], embt,
                                                     ALU.add, ALU.mult)
                            qk_dve[oc] = ctx.dinc(dve, "sDVE")
                            wt(gp, "GP", "sDVE", qk_dve[oc])
                            h = oc % 16
                            row = (h // 2) * 512 + (0 if oc < 16 else 256) + (h % 2) * 128
                            qk_st[oc] = st(qk_in[tau].ap()[row:row + 128, 0:1024],
                                           sqk[oc % 2][:])
                    with ExitStack() as pst4:
                        p4e = pst4.enter_context
                        wv = p4e(nc.sbuf_tensor(f"wv_{tau}", [128, FC * 512], BF16))
                        wvf = p4e(nc.sbuf_tensor(f"wvf_{tau}", [128, FC * 512], F32))
                        svb = [p4e(nc.sbuf_tensor(f"sv{i}_{tau}", [128, 512], BF16)) for i in range(2)]
                        pv = [p4e(nc.psum_tensor(f"pv{i}_{tau}", [128, 512], F32)) for i in range(2)]
                        v_pe, v_dve, v_st = {}, {}, {}
                        vcv = 0
                        for ncc in range(4):
                            wt(syn, "SY", "sDVE", vcv)
                            vlvf = ctx.lds("wvf", wvf[:].rearrange("p (kc n) -> p kc n", kc=16),
                                           qwfR[:, :, 4096 + ncc * 512: 4096 + (ncc + 1) * 512])
                            wt(dve, "DVE", "wvf", vlvf)
                            if ncc >= 1:
                                wt(dve, "DVE", "sPE", v_pe[(ncc - 1, 7)])
                            dve.tensor_copy(wv[:], wvf[:])
                            vcv = ctx.dinc(dve, "sDVE")
                            for tc in range(8):
                                wt(pe, "PE", "sDVE", vcv)
                                bi = tc % 2
                                prev = None
                                if (ncc, tc) not in ((0, 0), (0, 1)):
                                    prev = (ncc, tc - 2) if tc >= 2 else (ncc - 1, 6 + tc)
                                if prev is not None:
                                    wt(pe, "PE", "sDVE", v_dve[prev])
                                for kc in range(FC):
                                    pe.matmul(
                                        pv[bi][:],
                                        xn2[:, kc * 1024 + tc * 128: kc * 1024 + (tc + 1) * 128],
                                        wv[:, kc * 512:(kc + 1) * 512],
                                        start=(kc == 0), stop=(kc == FC - 1))
                                v_pe[(ncc, tc)] = ctx.dinc(pe, "sPE")
                                wt(dve, "DVE", "sPE", v_pe[(ncc, tc)])
                                if prev is not None:
                                    wt(dve, "DVE", "qG", v_st[prev])
                                dve.scalar_tensor_tensor(
                                    svb[bi][:], pv[bi][:], 1.0,
                                    vbias_bc[:, ncc * 512:(ncc + 1) * 512],
                                    ALU.mult, ALU.add)
                                v_dve[(ncc, tc)] = ctx.dinc(dve, "sDVE")
                                wt(gp, "GP", "sDVE", v_dve[(ncc, tc)])
                                rbase = tc * 128
                                st(v_in[tau].ap()[(2 * ncc) * 1024 + rbase:
                                                  (2 * ncc) * 1024 + rbase + 128, :],
                                   svb[bi][:][:, 0:256])
                                v_st[(ncc, tc)] = st(
                                    v_in[tau].ap()[(2 * ncc + 1) * 1024 + rbase:
                                                   (2 * ncc + 1) * 1024 + rbase + 128, :],
                                    svb[bi][:][:, 256:512])
            wt(gp, "GP", "qG", cnt["qG"])
            with allow_ici_all_to_all():
                ctx.inc(gp.collective_compute("AllToAll", ALU.bypass,
                                              replica_groups=[list(range(R))],
                                              ins=[qk_in[tau].ap().opt()],
                                              outs=[qk_att[tau].ap().opt()]),
                        "sCC")
                cc_qkv = ctx.inc(
                    gp.collective_compute("AllToAll", ALU.bypass,
                                          replica_groups=[list(range(R))],
                                          ins=[v_in[tau].ap().opt()],
                                          outs=[v_att[tau].ap().opt()]),
                    "sCC")
            ctx.fence()

        p3stack = ExitStack()
        sow_all = p3stack.enter_context(nc.sbuf_tensor("b_swall", [128, FC * 2048], BF16))
        wof = [p3stack.enter_context(nc.sbuf_tensor(f"b_wf{i}", [128, 2048], F32))
               for i in range(2)]
        sow_state = {"done": 0}

        qkA = [t.ap() for t in qk_att]
        vA = [t.ap() for t in v_att]
        oI = [t.ap() for t in o_in]
        with ExitStack() as ast:
            ae = ast.enter_context
            sqt = [ae(nc.sbuf_tensor(f"a_sq{i}", [128, 512], BF16)) for i in range(2)]
            skb = [ae(nc.sbuf_tensor(f"a_sk{i}", [128, 512], BF16)) for i in range(4)]
            svk = [ae(nc.sbuf_tensor(f"a_sv{i}", [128, 4 * 128], BF16)) for i in range(4)]
            PTa = ae(nc.sbuf_tensor("a_pt", [128, 4096], BF16))
            psSa = ae(nc.psum_tensor("a_psS", [128, 2048], F32))
            poT = ae(nc.psum_tensor("a_poT", [128, 512], F32))
            psL = ae(nc.psum_tensor("a_psL", [1, 512], F32))
            lbc = ae(nc.psum_tensor("a_lbc", [128, 512], F32))
            poTs = ae(nc.sbuf_tensor("a_poTs", [128, 512], F32))
            lrow = ae(nc.sbuf_tensor("a_lrow", [1, 512], F32))
            linvb = ae(nc.sbuf_tensor("a_linv", [128, 512], F32))
            osb = ae(nc.sbuf_tensor("a_osb", [128, 512], BF16))

            exp_last = {0: 0, 1: 0}
            avq_last = {0: 0, 1: 0, 2: 0, 3: 0}
            s_hist, av_hist, q_hist = {}, {}, {}
            state = {"evac": 0, "norm": 0, "ost": 0}
            tail = [None]
            pend = []

            def post_av(g):
                kb_, p_, qt_, gk_, hp, aw, _tl = g
                if not (kb_ == qt_ and p_ == 1):
                    return
                eg = ctx.dinc(pe, "sPE")
                wt(dve, "DVE", "sPE", eg)
                dve.tensor_copy(poTs[:], poT[:])
                dve.tensor_copy(lrow[:], psL[:])
                state["evac"] = ctx.dinc(dve, "sDVE")
                tail[0] = _tl

            def emit_tail():
                if tail[0] is None:
                    return
                rq_, hl_, qbase_ = tail[0]
                wt(pe, "PE", "sDVE", state["evac"])
                wt(pe, "PE", "sDVE", state["norm"])
                pe.matmul(lbc[:], ones1[:], lrow[:], start=True, stop=True)
                vB = ctx.dinc(pe, "sPE")
                wt(dve, "DVE", "sPE", vB)
                wt(dve, "DVE", "qG", state["ost"])
                dve.reciprocal(linvb[:], lbc[:])
                dve.tensor_tensor(osb[:], poTs[:], linvb[:], ALU.mult)
                state["norm"] = ctx.dinc(dve, "sDVE")
                wt(gp, "GP", "sDVE", state["norm"])
                orow = rq_ * 128
                state["ost"] = st(oI[hl_][orow:orow + 128, qbase_:qbase_ + 512], osb[:])
                tail[0] = None

            def emit_av(g):
                kb_, p_, qt_, gk_, hp, aw, _tl = g
                if kb_ == 0 and p_ == 0:
                    emit_tail()
                    wt(pe, "PE", "sDVE", state["evac"])
                wt(pe, "PE", aw["vs"], aw["vv"])
                wt(pe, "PE", aw["w"][0], aw["w"][1])
                svr_ = svk[gk_ % 4][:].rearrange("p (s n) -> p s n", s=4)
                pcol = hp * 1024
                mm = None
                for si in range(2):
                    s = 2 * p_ + si
                    pe.matmul(psL[:], onesb[:],
                              PTa[:, pcol + si * 512:pcol + (si + 1) * 512],
                              start=(kb_ == 0 and s == 0),
                              stop=(kb_ == qt_ and s == 3))
                for si in range(2):
                    s = 2 * p_ + si
                    mm = pe.matmul(poT[:], svr_[:, s],
                                   PTa[:, pcol + si * 512:pcol + (si + 1) * 512],
                                   start=(kb_ == 0 and s == 0),
                                   stop=(kb_ == qt_ and s == 3))
                avv = ctx.inc(mm, "sPE")
                avq_last[hp] = avv
                av_hist[gk_] = avv
                return avv

            it = 0
            gk = 0
            gq = 0
            q_last_S = {}
            cc_oh = {}
            for hl in range(2):
                for b in range(2):
                    for qt in range(16):
                        jq, qbase = key_loc(qt)
                        rq = 4 * b + jq
                        wt(syn, "SY", "sCC", cc_qkv)
                        if it >= 2:
                            wt(syn, "SY", "sPE", q_last_S.get(it - 2, 0))
                        q_hist[it] = ctx.lds(
                            f"sq{it % 2}", sqt[it % 2][:],
                            qkA[qbase // 1024][rq * 512 + hl * 128:
                                               rq * 512 + hl * 128 + 128,
                                qbase % 1024:qbase % 1024 + 512])
                        for kb in range(qt + 1):
                            jk, kbase = key_loc(kb)
                            rk = 4 * b + jk
                            krow = rk * 512 + 256 + hl * 128
                            ktau, kloc = kbase // 1024, kbase % 1024
                            vrow = rk * 1024 + kloc
                            if gk >= 4:
                                wt(syn, "SY", "sPE", s_hist.get(gk - 4, 0))
                                wt(syn, "SY", "sPE", av_hist.get(gk - 4, 0))
                            ctx.lds(f"kv{gk % 4}", skb[gk % 4][:],
                                    qkA[ktau][krow:krow + 128, kloc:kloc + 512])
                            vv = ctx.lds(f"kv{gk % 4}",
                                         svk[gk % 4][:].rearrange("p (s n) -> p s n", s=4),
                                         vA[ktau][vrow:vrow + 512, hl * 128:(hl + 1) * 128]
                                         .rearrange("(s p) n -> p s n", s=4, p=128))
                            wt(pe, "PE", f"kv{gk % 4}", vv)
                            wt(pe, "PE", f"sq{it % 2}", q_hist[it])
                            for p in range(2):
                                wt(pe, "PE", "sACT", exp_last[p])
                                for si in range(2):
                                    s = 2 * p + si
                                    pe.matmul(psSa[:, s * 512:(s + 1) * 512],
                                              skb[gk % 4][:][:, s * 128:(s + 1) * 128],
                                              sqt[it % 2][:], start=True, stop=True)
                            vS = ctx.dinc(pe, "sPE")
                            s_hist[gk] = vS
                            q_last_S[it] = vS
                            for p in range(2):
                                hp = gq % 4
                                gq += 1
                                pcol = hp * 1024
                                wt(act, "ACT", "sPE", vS)
                                wt(act, "ACT", "sPE", avq_last[hp])
                                act.activation(PTa[:, pcol:pcol + 1024],
                                               psSa[:, p * 1024:(p + 1) * 1024],
                                               AF.Exp, bias=shiftc[:], scale=1.0)
                                exp_last[p] = ctx.dinc(act, "sACT")
                                aw = {"vv": vv, "vs": f"kv{gk % 4}",
                                      "w": ("sACT", exp_last[p])}
                                if kb == qt:
                                    wt(dve, "DVE", "sACT", exp_last[p])
                                    dve.tensor_tensor(
                                        PTa[:, pcol:pcol + 1024],
                                        PTa[:, pcol:pcol + 1024],
                                        c_masks[:, p * 1024:(p + 1) * 1024], ALU.mult)
                                    aw = {"vv": vv, "vs": f"kv{gk % 4}",
                                          "w": ("sDVE", ctx.dinc(dve, "sDVE"))}
                                pend.append((kb, p, qt, gk, hp, aw, (rq, hl, qbase)))
                                if len(pend) > 2:
                                    g = pend.pop(0)
                                    emit_av(g)
                                    post_av(g)
                            gk += 1
                        it += 1
                if hl == 0:
                    while pend:
                        g = pend.pop(0)
                        emit_av(g)
                        post_av(g)
                    emit_tail()
                    socv = {}
                    for hd in range(16):
                        if hd >= 2:
                            wt(syn, "SY", "sDVE", socv[hd - 2])
                        vf = ctx.lds(f"wof{hd % 2}", wof[hd % 2][:], owfr[hd])
                        wt(dve, "DVE", f"wof{hd % 2}", vf)
                        dve.tensor_copy(sow_all[:, hd * 2048:(hd + 1) * 2048],
                                        wof[hd % 2][:])
                        socv[hd] = ctx.dinc(dve, "sDVE")
                    sow_state["done"] = socv[15]
                    wt(gp, "GP", "qG", cnt["qG"])
                    with allow_ici_all_to_all():
                        cc_oh[0] = ctx.inc(
                            gp.collective_compute("AllToAll", ALU.bypass,
                                                  replica_groups=[list(range(R))],
                                                  ins=[o_in[0].ap().opt()],
                                                  outs=[o_att[0].ap().opt()]),
                            "sCC")
            while pend:
                g = pend.pop(0)
                emit_av(g)
                post_av(g)
            emit_tail()

        wt(gp, "GP", "qG", cnt["qG"])
        with allow_ici_all_to_all():
            cc_o = ctx.inc(
                gp.collective_compute("AllToAll", ALU.bypass,
                                      replica_groups=[list(range(R))],
                                      ins=[o_in[1].ap().opt()], outs=[o_att[1].ap().opt()]),
                "sCC")

        oA = [t.ap() for t in o_att]
        att_pe_final = cnt["sPE"]
        att_dve_final = cnt["sDVE"]
        sow_done = sow_state["done"]
        cc_o0 = cc_oh[0]
        with ExitStack() as bst:
            be = bst.enter_context
            ofm = [be(nc.sbuf_tensor(f"b_of{i}", [128, FC * 512], BF16)) for i in range(2)]
            x1c = [be(nc.sbuf_tensor(f"b_x1{i}", [128, 512], F32)) for i in range(2)]
            x2c = [be(nc.sbuf_tensor(f"b_x2{i}", [128, 512], F32)) for i in range(2)]
            wt(syn, "SY", "sPE", att_pe_final)
            wt(syn, "SY", "sDVE", att_dve_final)
            with nc.psum_tensor("b_j0", [128, 512], F32) as bj0, \
                 nc.psum_tensor("b_j1", [128, 512], F32) as bj1:
                pj = [bj0, bj1]
                op_pe, op_dve, op_st = {}, {}, {}
                qb_last_pe, vofm = {}, {}
                for qb in range(4):
                    q0 = qb * 512
                    if qb >= 2:
                        wt(syn, "SY", "sPE", qb_last_pe[qb - 2])
                    wt(syn, "SY", "sCC", cc_o0)
                    for fc in range(0, 16, 2):
                        ctx.lds(f"of{qb % 2}",
                                ofm[qb % 2][:, fc * 512:(fc + 1) * 512],
                                oA[0][(fc // 2) * 128:(fc // 2) * 128 + 128,
                                      q0:q0 + 512])
                    wt(syn, "SY", "sCC", cc_o)
                    v = 0
                    for fc in range(1, 16, 2):
                        v = ctx.lds(f"of{qb % 2}",
                                    ofm[qb % 2][:, fc * 512:(fc + 1) * 512],
                                    oA[1][(fc // 2) * 128:(fc // 2) * 128 + 128,
                                          q0:q0 + 512])
                    vofm[qb] = v
                    for mc in range(16):
                        gi = qb * 16 + mc
                        if gi >= 2:
                            wt(syn, "SY", "sPE", op_pe[gi - 2])
                            wt(syn, "SY", "sDVE", op_dve[gi - 2])
                        vx1 = ctx.lds(f"so{gi % 2}", x1c[gi % 2][:],
                                      x1r[mc][:, q0:q0 + 512])
                        wt(pe, "PE", "sDVE", sow_done)
                        wt(pe, "PE", f"of{qb % 2}", vofm[qb])
                        if gi == 0:
                            wt(pe, "PE", "sPE", att_pe_final)
                            wt(pe, "PE", "sDVE", att_dve_final)
                        if gi >= 2:
                            wt(pe, "PE", "sDVE", op_dve[gi - 2])
                        for hd in range(16):
                            pe.matmul(pj[gi % 2][:],
                                      sow_all[:, hd * 2048 + mc * 128:
                                              hd * 2048 + (mc + 1) * 128],
                                      ofm[qb % 2][:, hd * 512:(hd + 1) * 512],
                                      start=(hd == 0), stop=(hd == 15))
                        op_pe[gi] = ctx.dinc(pe, "sPE")
                        wt(dve, "DVE", "sPE", op_pe[gi])
                        wt(dve, "DVE", f"so{gi % 2}", vx1)
                        if gi >= 2:
                            wt(dve, "DVE", "qG", op_st[gi - 2])
                        dve.scalar_tensor_tensor(x2c[gi % 2][:], pj[gi % 2][:],
                                                 c_outbT[:, mc:mc + 1], x1c[gi % 2][:],
                                                 ALU.add, ALU.add)
                        op_dve[gi] = ctx.dinc(dve, "sDVE")
                        wt(gp, "GP", "sDVE", op_dve[gi])
                        op_st[gi] = st(x2r[mc][:, q0:q0 + 512], x2c[gi % 2][:])
                    qb_last_pe[qb] = op_pe[qb * 16 + 15]
        p3stack.close()
        x2_qG = cnt["qG"]
        ctx.fence()

        emit_ffn(x2r, x2r, outr, "f2", cc_gu=cc_u, cc_dn=cc_d, dep_qG=x2_qG)

        if debug:
            dbg_pairs = [("dx1", x1D), ("dx2", x2D), ("dwg", wgD)]
            ctx.fence()
            for nm, ten in dbg_pairs:
                dd = dt_(nm, list(ten.shape), ten.dtype, kind="ExternalOutput")
                st(dd.ap(), ten.ap())

        wt(gp, "GP", "qG", cnt["qG"])
        wt(syn, "SY", "qG", cnt["qG"])
        wt(syn, "SY", "qS", cnt["qS"])
    return nc


def _host_prepare(inputs):
    x = np.asarray(inputs["L0_attn_x"], dtype=np.float32)
    qkv_w = np.asarray(inputs["L0_attn_qkv_w"], dtype=np.float32)
    qkv_b = np.asarray(inputs["L0_attn_qkv_b"], dtype=np.float32)
    out_w = np.asarray(inputs["L0_attn_out_w"], dtype=np.float32)
    out_b = np.asarray(inputs["L0_attn_out_b"], dtype=np.float32)
    lut = np.asarray(inputs["L0_ffn_lut"], dtype=np.float32)
    g_idx = np.asarray(inputs["L0_ffn_gate_idx"], dtype=np.int32)
    u_idx = np.asarray(inputs["L0_ffn_up_idx"], dtype=np.int32)
    d_idx = np.asarray(inputs["L0_ffn_down_idx"], dtype=np.int32)

    qkvwT = np.ascontiguousarray(qkv_w.T)
    qkvbT = np.ascontiguousarray(qkv_b.reshape(48, 128).T)
    vbias = np.ascontiguousarray(qkv_b[4096:].reshape(1, D))
    outwT = np.ascontiguousarray(out_w.T)
    outbT = np.ascontiguousarray(out_b.reshape(16, 128).T)
    lut2 = np.ascontiguousarray(lut.reshape(1, LUT))
    gT = np.ascontiguousarray(g_idx.T)
    uT = np.ascontiguousarray(u_idx.T)
    dT = np.ascontiguousarray(d_idx.T)

    freqs = np.arange(0, HD, 2, dtype=np.float32) / HD
    pos = np.arange(SEQ, dtype=np.float32)
    ang = pos[:, None] * freqs[None, :]
    emb = np.concatenate([np.sin(ang), np.cos(ang)], axis=-1)
    embT = np.ascontiguousarray(emb.T.astype(np.float32))

    kk = np.arange(128)[:, None]
    qq = np.arange(512)[None, :]
    masks = np.concatenate(
        [(128 * m + kk <= qq) for m in range(4)], axis=1).astype(ml_dtypes.bfloat16)

    in_maps, metas = [], []
    for cid in range(R):
        bb = cid // 4
        jj = cid % 4
        tok = np.concatenate([np.arange(ca * 1024, (ca + 1) * 1024) for ca in (jj, 7 - jj)])
        im = {
            "xT": np.ascontiguousarray(x[bb][tok].T),
            "qkvwT": qkvwT, "qkvbT": qkvbT, "vbias": vbias,
            "outwT": outwT, "outbT": outbT, "lut": lut2,
            "gidx": np.ascontiguousarray(gT[cid * 256:(cid + 1) * 256]),
            "uidx": np.ascontiguousarray(uT[cid * 256:(cid + 1) * 256]),
            "didx": np.ascontiguousarray(dT[:, cid * 256:(cid + 1) * 256]),
            "embq": np.ascontiguousarray(embT[:, tok] * np.float32(1.0 / np.sqrt(HD))),
            "embk": np.ascontiguousarray(embT[:, tok]),
            "masks": masks,
        }
        in_maps.append(im)
        metas.append((bb, tok))
    return in_maps, metas


def kernel(**inputs):
    in_maps, metas = _host_prepare(inputs)
    nc = build()
    res = run_bass_kernel_spmd(nc, in_maps, core_ids=list(range(R)))
    out = np.empty((B, SEQ, D), dtype=np.float32)
    for cid in range(R):
        bb, tok = metas[cid]
        out[bb, tok, :] = np.asarray(res.results[cid]["out"]).T
    return out


# version 46
# speedup vs baseline: 1.0025x; 1.0025x over previous
import numpy as np
import ml_dtypes
import concourse.bass as bass
import concourse.mybir as mybir
from concourse.bass_utils import run_bass_kernel_spmd
from concourse.aot_env import allow_ici_all_to_all
from contextlib import ExitStack

F32 = mybir.dt.float32
BF16 = mybir.dt.bfloat16
I32 = mybir.dt.int32
AF = mybir.ActivationFunctionType
ALU = mybir.AluOpType

R = 8
B, SEQ, D = 2, 8192, 2048
H, HD = 16, 128
T = 2048
FC = 16
INTER = 5504
OC = 43
LUT = 16
EXP_SHIFT = 8.0
INV_D = 1.0 / 2048.0


class Ctx:
    def __init__(self, nc, stack):
        self.nc = nc
        self.stack = stack
        self.c = {}
        self.sem = {}
        self.last_wait = {}
        for name in ("qS", "qG", "sCC", "sPE", "sACT", "sDVE"):
            self.sem[name] = stack.enter_context(nc.semaphore(name))
            self.c[name] = 0

    def _sem(self, name):
        if name not in self.sem:
            self.sem[name] = self.stack.enter_context(self.nc.semaphore(name))
            self.c[name] = 0
        return self.sem[name]

    def lds(self, sname, dst, src):
        s = self._sem(sname)
        instr = self.nc.sync.dma_start(dst, src)
        instr.then_inc(s, 16)
        self.c[sname] += 16
        return self.c[sname]

    def lda(self, sname, dst, src):
        s = self._sem(sname)
        instr = self.nc.scalar.dma_start(dst, src)
        instr.then_inc(s, 16)
        self.c[sname] += 16
        return self.c[sname]

    def inc(self, instr, name, n=1):
        instr.then_inc(self.sem[name], n)
        self.c[name] += n
        return self.c[name]

    def dinc(self, eng, name, n=1):
        return self.inc(eng.drain(), name, n)

    def ld(self, dst, src):
        return self.inc(self.nc.sync.dma_start(dst, src), "qS", 16)

    def st(self, dst, src):
        return self.inc(self.nc.gpsimd.dma_start(dst, src), "qG", 16)

    def wt(self, eng, ename, sname, v):
        if v is None or v <= 0:
            return
        self._sem(sname)
        if sname in ("qS", "qG"):
            v = self.c[sname]
        key = (ename, sname)
        if self.last_wait.get(key, 0) >= v:
            return
        eng.wait_ge(self.sem[sname], v)
        self.last_wait[key] = v

    def fence(self):
        nc = self.nc
        engs = [(nc.sync, "SY"), (nc.gpsimd, "GP"), (nc.tensor, "PE"),
                (nc.scalar, "ACT"), (nc.vector, "DVE")]
        own = {"PE": "sPE", "ACT": "sACT", "DVE": "sDVE"}
        snap = dict(self.c)
        for eng, en in engs:
            for sname in ("qS", "qG", "sPE", "sACT", "sDVE"):
                if own.get(en) == sname:
                    continue
                self.wt(eng, en, sname, snap[sname])


def key_loc(kb):
    kc = kb // 2
    jk = min(kc, 7 - kc)
    base = (0 if kc <= 3 else 1024) + (kb % 2) * 512
    return jk, base


def build(debug=False):
    nc = bass.Bass("TRN2", target_bir_lowering=False)
    dt_ = nc.dram_tensor
    xT = dt_("xT", [D, T], F32, kind="ExternalInput").ap()
    qkvwT = dt_("qkvwT", [D, 3 * D], F32, kind="ExternalInput").ap()
    qkvbT = dt_("qkvbT", [128, 48], F32, kind="ExternalInput").ap()
    vbiasI = dt_("vbias", [1, D], F32, kind="ExternalInput").ap()
    outwT = dt_("outwT", [D, D], F32, kind="ExternalInput").ap()
    outbT = dt_("outbT", [128, 16], F32, kind="ExternalInput").ap()
    lutI = dt_("lut", [1, LUT], F32, kind="ExternalInput").ap()
    gidx = dt_("gidx", [256, INTER], I32, kind="ExternalInput").ap()
    uidx = dt_("uidx", [256, INTER], I32, kind="ExternalInput").ap()
    didx = dt_("didx", [INTER, 256], I32, kind="ExternalInput").ap()
    embqI = dt_("embq", [128, T], F32, kind="ExternalInput").ap()
    embkI = dt_("embk", [128, T], F32, kind="ExternalInput").ap()
    masksI = dt_("masks", [128, 4 * 512], BF16, kind="ExternalInput").ap()
    outD = dt_("out", [D, T], F32, kind="ExternalOutput").ap()

    wrm_i = dt_("wrm_i", [8, 128], BF16)
    wrm_o = dt_("wrm_o", [64, 128], BF16)
    wg_sh = dt_("wg_sh", [2 * OC * 128, 128], BF16)
    wu_sh = dt_("wu_sh", [2 * OC * 128, 128], BF16)
    wd_sh = dt_("wd_sh", [2 * OC * 128, 128], BF16)
    wgD = dt_("wg", [16 * OC * 128, 128], BF16)
    wuD = dt_("wu", [16 * OC * 128, 128], BF16)
    wdD = dt_("wd", [16 * OC * 128, 128], BF16)
    qk_in = [dt_(f"qk_in{i}", [R * 512, 1024], BF16) for i in range(2)]
    qk_att = [dt_(f"qk_att{i}", [R * 512, 1024], BF16) for i in range(2)]
    v_in = [dt_(f"v_in{i}", [R * 1024, 256], BF16) for i in range(2)]
    v_att = [dt_(f"v_att{i}", [R * 1024, 256], BF16) for i in range(2)]
    o_in = [dt_(f"o_in{i}", [R * 128, T], BF16) for i in range(2)]
    o_att = [dt_(f"o_att{i}", [R * 128, T], BF16) for i in range(2)]
    x1D = dt_("x1", [D, T], F32)
    x2D = dt_("x2", [D, T], F32)

    with ExitStack() as stack:
        e = stack.enter_context
        ctx = Ctx(nc, stack)
        cnt = ctx.c
        wt, ld, st = ctx.wt, ctx.ld, ctx.st
        syn, gp, pe, act, dve = nc.sync, nc.gpsimd, nc.tensor, nc.scalar, nc.vector

        ctx.inc(gp.collective_compute("AllGather", ALU.bypass,
                                      replica_groups=[list(range(R))],
                                      ins=[wrm_i.ap().opt()],
                                      outs=[wrm_o.ap().opt()]), "sCC")

        c_masks = e(nc.sbuf_tensor("c_masks", [128, 4 * 512], BF16))
        c_qkvbT = e(nc.sbuf_tensor("c_qkvbT", [128, 48], F32))
        c_outbT = e(nc.sbuf_tensor("c_outbT", [128, 16], F32))
        c_lut = e(nc.sbuf_tensor("c_lut", [1, LUT], F32))
        c_vbias = e(nc.sbuf_tensor("c_vbias", [1, D], F32))
        ones1 = e(nc.sbuf_tensor("ones1", [1, 128], F32))
        onesb = e(nc.sbuf_tensor("onesb", [128, 1], BF16))
        lutbc = e(nc.sbuf_tensor("lutbc", [128, LUT], F32))
        lutbch = e(nc.sbuf_tensor("lutbch", [128, LUT], F32))
        vbias_bc = e(nc.sbuf_tensor("vbias_bc", [128, D], BF16))
        epsc = e(nc.sbuf_tensor("epsc", [1, 1], F32))
        shiftc = e(nc.sbuf_tensor("shiftc", [128, 1], F32))

        ctx.lds("cst", c_masks[:], masksI)
        ctx.lds("cst", c_qkvbT[:], qkvbT)
        ctx.lds("cst", c_outbT[:], outbT)
        ctx.lds("cst", c_lut[:], lutI)
        consts_loaded = ctx.lds("cst", c_vbias[:], vbiasI)

        dve.memset(ones1[:], 1.0)
        dve.memset(onesb[:], 1.0)
        dve.memset(epsc[:], 1e-5)
        with nc.psum_tensor("ps_lbc", [128, LUT], F32) as ps_lbc, \
             nc.psum_tensor("ps_vb", [128, D], F32) as ps_vb:
            dve.memset(shiftc[:], -EXP_SHIFT)
            msets = ctx.dinc(dve, "sDVE")
            wt(pe, "PE", "cst", consts_loaded)
            wt(pe, "PE", "sDVE", msets)
            pe.matmul(ps_lbc[:], ones1[:], c_lut[:], start=True, stop=True)
            for i in range(4):
                pe.matmul(ps_vb[:, i * 512:(i + 1) * 512], ones1[:],
                          c_vbias[:, i * 512:(i + 1) * 512], start=True, stop=True)
            v0 = ctx.dinc(pe, "sPE")
            wt(dve, "DVE", "sPE", v0)
            dve.tensor_copy(lutbc[:], ps_lbc[:])
            dve.tensor_scalar(lutbch[:], ps_lbc[:], 0.5, None, ALU.mult)
            dve.tensor_copy(vbias_bc[:], ps_vb[:])
            consts_done = ctx.dinc(dve, "sDVE")

        with nc.sbuf_tensor("m_idx", [128, 2 * INTER], I32) as m_idx, \
             nc.sbuf_tensor("m_idxb", [128, 2 * INTER], BF16) as m_idxb, \
             nc.sbuf_tensor("m_acc", [128, 2 * INTER], BF16) as m_acc, \
             nc.sbuf_tensor("m_term", [128, 2 * INTER], BF16) as m_term:

            mat_prev = {"vd": 0, "stores": 0}

            def materialize(kind, idx_ap, sh, full, lcol):
                wt(syn, "SY", "sDVE", mat_prev["vd"])
                if kind == "g":
                    vl = ctx.lds("mix", m_idx[:].rearrange("p (l o) -> p l o", l=2),
                                 idx_ap.rearrange("(l p) o -> p l o", l=2, p=128))
                else:
                    vl = ctx.lds("mix", m_idx[:].rearrange("p (i m) -> p i m", i=OC),
                                 idx_ap.rearrange("(i p) m -> p i m", i=OC, p=128))
                wt(dve, "DVE", "mix", vl)
                wt(dve, "DVE", "sDVE", consts_done)
                wt(dve, "DVE", "qG", mat_prev["stores"])
                dve.tensor_copy(m_idxb[:], m_idx[:])
                for v in range(LUT):
                    dve.tensor_scalar(m_term[:], m_idxb[:], float(v),
                                      lcol[:, v:v + 1], ALU.is_equal, ALU.mult)
                    if v == 0:
                        dve.tensor_copy(m_acc[:], m_term[:])
                    else:
                        dve.tensor_tensor(m_acc[:], m_acc[:], m_term[:], ALU.add)
                vd = ctx.dinc(dve, "sDVE")
                mat_prev["vd"] = vd
                wt(gp, "GP", "sDVE", vd)
                shr = sh.ap().rearrange("(l o p) n -> l p o n", l=2, o=OC, p=128)
                if kind == "g":
                    srcr = m_acc[:].rearrange("p (l o n) -> p l o n", l=2, o=OC, n=128)
                    srcs = [srcr[:, 0], srcr[:, 1]]
                else:
                    srcr = m_acc[:].rearrange("p (i l n) -> p i l n", i=OC, l=2, n=128)
                    srcs = [srcr[:, :, 0], srcr[:, :, 1]]
                for blk in range(2):
                    st(shr[blk], srcs[blk])
                mat_prev["stores"] = cnt["qG"]
                wt(gp, "GP", "qG", cnt["qG"])
                return ctx.inc(
                    gp.collective_compute("AllGather", ALU.bypass,
                                          replica_groups=[list(range(R))],
                                          ins=[sh.ap().opt()], outs=[full.ap().opt()]),
                    "sCC")

            cc_g = materialize("g", gidx, wg_sh, wgD, lutbc)
            cc_u = materialize("g", uidx, wu_sh, wuD, lutbc)
            cc_d = materialize("d", didx, wd_sh, wdD, lutbch)

        ctx.fence()


        def emit_ln(src_r, xn, tau, tag, dep_qG=0):
            v_norm = 0
            with ExitStack() as lst:
                le = lst.enter_context
                xb = le(nc.sbuf_tensor(f"lx_{tag}", [128, FC * 512], BF16))
                xl = [le(nc.sbuf_tensor(f"ll{i}_{tag}", [128, 512], F32)) for i in range(4)]
                sq2 = [le(nc.sbuf_tensor(f"lq{i}_{tag}", [128, 512], BF16)) for i in range(2)]
                sts = le(nc.sbuf_tensor(f"ls_{tag}", [1, 512 * 6], F32))
                bcb = le(nc.sbuf_tensor(f"lb_{tag}", [128, 1024], BF16))
                mean, ex2 = sts[:, 0:512], sts[:, 512:1024]
                var, lnv = sts[:, 1024:1536], sts[:, 1536:2048]
                rr, mr = sts[:, 2048:2560], sts[:, 2560:3072]
                prev_pe = 0
                prev_norm = 0
                prev_cv_last = 0
                g_norm = [0, 0]
                for g in range(2):
                    t0 = tau * 1024 + g * 512
                    with nc.psum_tensor(f"lps_{tag}{g}", [1, 512], F32) as ps_su, \
                         nc.psum_tensor(f"lpq_{tag}{g}", [1, 512], F32) as ps_sq, \
                         nc.psum_tensor(f"lpb_{tag}{g}", [128, 1024], F32) as ps_b:
                        cvs, sqv, smm = {}, {}, {}
                        wt(syn, "SY", "qG", dep_qG)
                        for kc in range(FC):
                            if kc >= 4:
                                wt(syn, "SY", "sDVE", cvs[kc - 4])
                            else:
                                wt(syn, "SY", "sDVE", prev_cv_last)
                            vl = ctx.lds(f"xl{kc % 4}", xl[kc % 4][:],
                                         src_r[kc][:, t0:t0 + 512])
                            wt(dve, "DVE", f"xl{kc % 4}", vl)
                            if kc == 0:
                                wt(dve, "DVE", "sPE", prev_pe)
                            dve.tensor_copy(xb[:, kc * 512:(kc + 1) * 512], xl[kc % 4][:])
                            cvs[kc] = ctx.dinc(dve, "sDVE")
                            wt(act, "ACT", "sDVE", cvs[kc])
                            if kc == 0:
                                wt(act, "ACT", "sPE", prev_pe)
                            if kc >= 2:
                                wt(act, "ACT", "sPE", smm[kc - 2])
                            act.square(sq2[kc % 2][:],
                                       xb[:, kc * 512:(kc + 1) * 512])
                            sqv[kc] = ctx.dinc(act, "sACT")
                            wt(pe, "PE", "sACT", sqv[kc])
                            pe.matmul(ps_su[:], onesb[:], xb[:, kc * 512:(kc + 1) * 512],
                                      start=(kc == 0), stop=(kc == FC - 1))
                            mmq = pe.matmul(ps_sq[:], onesb[:], sq2[kc % 2][:],
                                            start=(kc == 0), stop=(kc == FC - 1))
                            smm[kc] = ctx.inc(mmq, "sPE")
                        prev_cv_last = cvs[FC - 1]
                        prev_pe = ctx.dinc(pe, "sPE")
                        wt(dve, "DVE", "sPE", prev_pe)
                        dve.tensor_scalar(mean, ps_su[:], INV_D, None, ALU.mult)
                        dve.tensor_scalar(ex2, ps_sq[:], INV_D, None, ALU.mult)
                        dve.tensor_tensor(var, mean, mean, ALU.mult)
                        dve.tensor_tensor(var, ex2, var, ALU.subtract)
                        v_var = ctx.dinc(dve, "sDVE")
                        wt(act, "ACT", "sDVE", v_var)
                        act.activation(lnv, var, AF.Ln, bias=epsc[:], scale=1.0)
                        act.activation(rr, lnv, AF.Exp, scale=-0.5)
                        v_r = ctx.dinc(act, "sACT")
                        wt(dve, "DVE", "sACT", v_r)
                        dve.tensor_tensor(mr, mean, rr, ALU.mult)
                        v_mr = ctx.dinc(dve, "sDVE")
                        wt(pe, "PE", "sDVE", v_mr)
                        wt(pe, "PE", "sDVE", prev_norm)
                        pe.matmul(ps_b[:, 0:512], ones1[:], rr, start=True, stop=True)
                        pe.matmul(ps_b[:, 512:1024], ones1[:], mr, start=True, stop=True)
                        v_bc = ctx.dinc(pe, "sPE")
                        wt(dve, "DVE", "sPE", v_bc)
                        dve.tensor_copy(bcb[:], ps_b[:])
                        for kc in range(FC):
                            col = kc * 1024 + g * 512
                            dve.tensor_tensor(xn[:, col:col + 512],
                                              xb[:, kc * 512:(kc + 1) * 512],
                                              bcb[:, 0:512], ALU.mult)
                            dve.tensor_tensor(xn[:, col:col + 512],
                                              xn[:, col:col + 512],
                                              bcb[:, 512:1024], ALU.subtract)
                        prev_norm = v_norm = ctx.dinc(dve, "sDVE")
                        g_norm[g] = v_norm
            return g_norm

        wgR = wgD.ap().rearrange("(kc o p) n -> o p kc n", kc=16, o=OC, p=128)
        wuR = wuD.ap().rearrange("(kc o p) n -> o p kc n", kc=16, o=OC, p=128)
        wdR = wdD.ap().rearrange("(mc i p) n -> mc p i n", mc=16, i=OC, p=128)
        xTr = xT.rearrange("(kc p) t -> kc p t", kc=FC, p=128)
        x1r = x1D.ap().rearrange("(kc p) t -> kc p t", kc=FC, p=128)
        x2r = x2D.ap().rearrange("(kc p) t -> kc p t", kc=FC, p=128)
        outr = outD.rearrange("(kc p) t -> kc p t", kc=FC, p=128)
        qwfR = qkvwT.rearrange("(kc p) c -> p kc c", kc=FC, p=128)
        owfr = outwT.rearrange("(kc p) c -> kc p c", kc=FC, p=128)

        def emit_ffn(src_r, res_r, dst_r, tag, cc_gu, cc_dn, dep_qG=0):
            for tau in range(2):
                ts = tau * 1024
                with ExitStack() as fst:
                    fe = fst.enter_context
                    xn = fe(nc.sbuf_tensor(f"xn_{tag}{tau}", [128, FC * 1024], BF16))
                    wbg = [fe(nc.sbuf_tensor(f"wbg{i}_{tag}{tau}", [128, 2048], BF16)) for i in range(3)]
                    wbu = [fe(nc.sbuf_tensor(f"wbu{i}_{tag}{tau}", [128, 2048], BF16)) for i in range(3)]
                    hsil = [fe(nc.sbuf_tensor(f"hs{i}_{tag}{tau}", [128, 1024], BF16)) for i in range(2)]
                    hbuf = fe(nc.sbuf_tensor(f"h_{tag}{tau}", [128, OC * 1024], BF16))
                    v_xng = emit_ln(src_r, xn, tau, f"{tag}{tau}", dep_qG=dep_qG)
                    v_xn = v_xng[1]
                    if True:
                        pe_done, silu_done, hm_done = {}, {}, {}
                        with nc.psum_tensor(f"pg0_{tag}{tau}", [128, 1024], F32) as pg0, \
                             nc.psum_tensor(f"pg1_{tag}{tau}", [128, 1024], F32) as pg1, \
                             nc.psum_tensor(f"pu0_{tag}{tau}", [128, 1024], F32) as pu0, \
                             nc.psum_tensor(f"pu1_{tag}{tau}", [128, 1024], F32) as pu1:
                            pg, pu = [pg0, pg1], [pu0, pu1]
                            for oc in range(OC):
                                if oc >= 3:
                                    wt(syn, "SY", "sPE", pe_done[oc - 3])
                                else:
                                    wt(syn, "SY", "sCC", cc_gu)
                                ctx.lds(f"gu{oc % 3}",
                                        wbg[oc % 3][:].rearrange("p (kc n) -> p kc n", kc=16),
                                        wgR[oc])
                                vlu = ctx.lds(f"gu{oc % 3}",
                                              wbu[oc % 3][:].rearrange("p (kc n) -> p kc n", kc=16),
                                              wuR[oc])
                                wt(pe, "PE", f"gu{oc % 3}", vlu)
                                if oc >= 2:
                                    wt(pe, "PE", "sACT", silu_done[oc - 2])
                                    wt(pe, "PE", "sDVE", hm_done[oc - 2])
                                for hf in range(2):
                                    wt(pe, "PE", "sDVE", v_xng[hf])
                                    for buf, ps in ((wbg, pg), (wbu, pu)):
                                        for kc in range(FC):
                                            pe.matmul(
                                                ps[oc % 2][:, hf * 512:(hf + 1) * 512],
                                                buf[oc % 3][:, kc * 128:(kc + 1) * 128],
                                                xn[:, kc * 1024 + hf * 512:
                                                   kc * 1024 + (hf + 1) * 512],
                                                start=(kc == 0), stop=(kc == FC - 1))
                                pe_done[oc] = ctx.dinc(pe, "sPE")
                                wt(act, "ACT", "sPE", pe_done[oc])
                                if oc >= 2:
                                    wt(act, "ACT", "sDVE", hm_done[oc - 2])
                                act.activation(hsil[oc % 2][:], pg[oc % 2][:], AF.Silu)
                                silu_done[oc] = ctx.dinc(act, "sACT")
                                wt(dve, "DVE", "sACT", silu_done[oc])
                                dve.tensor_tensor(hbuf[:, oc * 1024:(oc + 1) * 1024],
                                                  pu[oc % 2][:], hsil[oc % 2][:], ALU.mult)
                                hm_done[oc] = ctx.dinc(dve, "sDVE")
                        with nc.sbuf_tensor(f"wdh0_{tag}{tau}", [128, 22 * 128], BF16) as wdh0, \
                             nc.sbuf_tensor(f"wdh1_{tag}{tau}", [128, 22 * 128], BF16) as wdh1, \
                             nc.sbuf_tensor(f"wdh2_{tag}{tau}", [128, 22 * 128], BF16) as wdh2, \
                             nc.sbuf_tensor(f"xr0_{tag}{tau}", [128, 1024], F32) as xr0, \
                             nc.sbuf_tensor(f"xr1_{tag}{tau}", [128, 1024], F32) as xr1, \
                             nc.sbuf_tensor(f"so0_{tag}{tau}", [128, 1024], F32) as so0, \
                             nc.sbuf_tensor(f"so1_{tag}{tau}", [128, 1024], F32) as so1, \
                             nc.psum_tensor(f"pd0_{tag}{tau}", [128, 1024], F32) as pd0, \
                             nc.psum_tensor(f"pd1_{tag}{tau}", [128, 1024], F32) as pd1:
                            wdh, xres, sx1 = [wdh0, wdh1, wdh2], [xr0, xr1], [so0, so1]
                            pd = [pd0, pd1]
                            halves = [(0, 22), (22, 43)]
                            dstt, stst, hload = {}, {}, {}
                            for mc in range(16):
                                for hi, (i0, i1) in enumerate(halves):
                                    hidx = mc * 2 + hi
                                    if hidx >= 3:
                                        wt(syn, "SY", "sPE", pe_done[("d", hidx - 3)])
                                    else:
                                        wt(syn, "SY", "sCC", cc_dn)
                                    hload[hidx] = ctx.lds(
                                        f"dh{hidx % 3}",
                                        wdh[hidx % 3][:][:, 0:(i1 - i0) * 128].rearrange(
                                            "p (i n) -> p i n", n=128),
                                        wdR[mc][:, i0:i1, :])
                                    wt(pe, "PE", f"dh{hidx % 3}", hload[hidx])
                                    if mc == 0 and hi == 0:
                                        wt(pe, "PE", "sDVE", hm_done[OC - 1])
                                    if mc >= 2 and hi == 0:
                                        wt(pe, "PE", "sDVE", dstt[mc - 2])
                                    for i in range(i0, i1):
                                        for hf in range(2):
                                            pe.matmul(
                                                pd[mc % 2][:, hf * 512:(hf + 1) * 512],
                                                wdh[hidx % 3][:][:, (i - i0) * 128:(i - i0 + 1) * 128],
                                                hbuf[:, i * 1024 + hf * 512:
                                                     i * 1024 + (hf + 1) * 512],
                                                start=(i == 0), stop=(i == OC - 1))
                                    pe_done[("d", hidx)] = ctx.dinc(pe, "sPE")
                                if mc >= 2:
                                    wt(syn, "SY", "sDVE", dstt[mc - 2])
                                vxr = ctx.lds(f"xr{mc % 2}", xres[mc % 2][:],
                                              res_r[mc][:, ts:ts + 1024])
                                wt(dve, "DVE", "sPE", pe_done[("d", mc * 2 + 1)])
                                wt(dve, "DVE", f"xr{mc % 2}", vxr)
                                if mc >= 2:
                                    wt(dve, "DVE", "qG", stst[mc - 2])
                                dve.scalar_tensor_tensor(sx1[mc % 2][:], pd[mc % 2][:], 1.0,
                                                         xres[mc % 2][:], ALU.mult, ALU.add)
                                dstt[mc] = ctx.dinc(dve, "sDVE")
                                wt(gp, "GP", "sDVE", dstt[mc])
                                stst[mc] = st(dst_r[mc][:, ts:ts + 1024], sx1[mc % 2][:])
                    ctx.fence()
            return cnt["qG"]

        x1_qG = emit_ffn(xTr, xTr, x1r, "f1", cc_gu=cc_u, cc_dn=cc_d, dep_qG=0)

        for tau in range(2):
            ts = tau * 1024
            with ExitStack() as pst2:
                p2e = pst2.enter_context
                xn2 = p2e(nc.sbuf_tensor(f"xn2_{tau}", [128, FC * 1024], BF16))
                wqk = [p2e(nc.sbuf_tensor(f"wqk{i}_{tau}", [128, 2048], BF16)) for i in range(3)]
                wqf = [p2e(nc.sbuf_tensor(f"wqf{i}_{tau}", [128, 2048], F32)) for i in range(3)]
                sqk = [p2e(nc.sbuf_tensor(f"sqk{i}_{tau}", [128, 1024], BF16)) for i in range(2)]
                embB = [p2e(nc.sbuf_tensor(f"emb{i}_{tau}", [128, 1024], F32)) for i in range(2)]
                v_xn2g = emit_ln(x1r, xn2, tau, f"q{tau}", dep_qG=x1_qG)
                v_xn2 = v_xn2g[1]
                if True:
                    qcv = {}

                    def qk_conv(j, qk_pe):
                        if j >= 32:
                            return
                        if j >= 3:
                            wt(syn, "SY", "sDVE", qcv[j - 3])
                        vlf = ctx.lds(f"qwf{j % 3}",
                                      wqf[j % 3][:].rearrange("p (kc n) -> p kc n", kc=16),
                                      qwfR[:, :, j * 128:(j + 1) * 128])
                        wt(dve, "DVE", f"qwf{j % 3}", vlf)
                        if j >= 3:
                            wt(dve, "DVE", "sPE", qk_pe.get(j - 3, 0))
                        dve.tensor_copy(wqk[j % 3][:], wqf[j % 3][:])
                        qcv[j] = ctx.dinc(dve, "sDVE")
                    v_eq = ctx.lds("emb", embB[0][:], embqI[:, ts:ts + 1024])
                    v_ek = ctx.lds("emb", embB[1][:], embkI[:, ts:ts + 1024])
                    with ExitStack() as pst3:
                        pqk = [pst3.enter_context(nc.psum_tensor(f"pqk{i}_{tau}", [128, 1024], F32)) for i in range(2)]
                        qk_pe, qk_dve, qk_st = {}, {}, {}
                        qk_conv(0, qk_pe)
                        qk_conv(1, qk_pe)
                        qk_conv(2, qk_pe)
                        for oc in range(32):
                            wt(pe, "PE", "sDVE", qcv[oc])
                            if oc >= 2:
                                wt(pe, "PE", "sDVE", qk_dve[oc - 2])
                            for hf in range(2):
                                wt(pe, "PE", "sDVE", v_xn2g[hf])
                                for kc in range(FC):
                                    pe.matmul(
                                        pqk[oc % 2][:, hf * 512:(hf + 1) * 512],
                                        wqk[oc % 3][:][:, kc * 128:(kc + 1) * 128],
                                        xn2[:, kc * 1024 + hf * 512: kc * 1024 + (hf + 1) * 512],
                                        start=(kc == 0), stop=(kc == FC - 1))
                            qk_pe[oc] = ctx.dinc(pe, "sPE")
                            qk_conv(oc + 3, qk_pe)
                            wt(dve, "DVE", "sPE", qk_pe[oc])
                            wt(dve, "DVE", "emb", max(v_eq, v_ek))
                            if oc >= 2:
                                wt(dve, "DVE", "qG", qk_st[oc - 2])
                            embt = embB[0][:] if oc < 16 else embB[1][:]
                            dve.scalar_tensor_tensor(sqk[oc % 2][:], pqk[oc % 2][:],
                                                     c_qkvbT[:, oc:oc + 1], embt,
                                                     ALU.add, ALU.mult)
                            qk_dve[oc] = ctx.dinc(dve, "sDVE")
                            wt(gp, "GP", "sDVE", qk_dve[oc])
                            h = oc % 16
                            row = (h // 2) * 512 + (0 if oc < 16 else 256) + (h % 2) * 128
                            qk_st[oc] = st(qk_in[tau].ap()[row:row + 128, 0:1024],
                                           sqk[oc % 2][:])
                    with ExitStack() as pst4:
                        p4e = pst4.enter_context
                        wv = p4e(nc.sbuf_tensor(f"wv_{tau}", [128, FC * 512], BF16))
                        wvf = p4e(nc.sbuf_tensor(f"wvf_{tau}", [128, FC * 512], F32))
                        svb = [p4e(nc.sbuf_tensor(f"sv{i}_{tau}", [128, 512], BF16)) for i in range(2)]
                        pv = [p4e(nc.psum_tensor(f"pv{i}_{tau}", [128, 512], F32)) for i in range(2)]
                        v_pe, v_dve, v_st = {}, {}, {}
                        vcv = 0
                        for ncc in range(4):
                            wt(syn, "SY", "sDVE", vcv)
                            vlvf = ctx.lds("wvf", wvf[:].rearrange("p (kc n) -> p kc n", kc=16),
                                           qwfR[:, :, 4096 + ncc * 512: 4096 + (ncc + 1) * 512])
                            wt(dve, "DVE", "wvf", vlvf)
                            if ncc >= 1:
                                wt(dve, "DVE", "sPE", v_pe[(ncc - 1, 7)])
                            dve.tensor_copy(wv[:], wvf[:])
                            vcv = ctx.dinc(dve, "sDVE")
                            for tc in range(8):
                                wt(pe, "PE", "sDVE", vcv)
                                bi = tc % 2
                                prev = None
                                if (ncc, tc) not in ((0, 0), (0, 1)):
                                    prev = (ncc, tc - 2) if tc >= 2 else (ncc - 1, 6 + tc)
                                if prev is not None:
                                    wt(pe, "PE", "sDVE", v_dve[prev])
                                for kc in range(FC):
                                    pe.matmul(
                                        pv[bi][:],
                                        xn2[:, kc * 1024 + tc * 128: kc * 1024 + (tc + 1) * 128],
                                        wv[:, kc * 512:(kc + 1) * 512],
                                        start=(kc == 0), stop=(kc == FC - 1))
                                v_pe[(ncc, tc)] = ctx.dinc(pe, "sPE")
                                wt(dve, "DVE", "sPE", v_pe[(ncc, tc)])
                                if prev is not None:
                                    wt(dve, "DVE", "qG", v_st[prev])
                                dve.scalar_tensor_tensor(
                                    svb[bi][:], pv[bi][:], 1.0,
                                    vbias_bc[:, ncc * 512:(ncc + 1) * 512],
                                    ALU.mult, ALU.add)
                                v_dve[(ncc, tc)] = ctx.dinc(dve, "sDVE")
                                wt(gp, "GP", "sDVE", v_dve[(ncc, tc)])
                                rbase = tc * 128
                                st(v_in[tau].ap()[(2 * ncc) * 1024 + rbase:
                                                  (2 * ncc) * 1024 + rbase + 128, :],
                                   svb[bi][:][:, 0:256])
                                v_st[(ncc, tc)] = st(
                                    v_in[tau].ap()[(2 * ncc + 1) * 1024 + rbase:
                                                   (2 * ncc + 1) * 1024 + rbase + 128, :],
                                    svb[bi][:][:, 256:512])
            wt(gp, "GP", "qG", cnt["qG"])
            with allow_ici_all_to_all():
                ctx.inc(gp.collective_compute("AllToAll", ALU.bypass,
                                              replica_groups=[list(range(R))],
                                              ins=[qk_in[tau].ap().opt()],
                                              outs=[qk_att[tau].ap().opt()]),
                        "sCC")
                cc_qkv = ctx.inc(
                    gp.collective_compute("AllToAll", ALU.bypass,
                                          replica_groups=[list(range(R))],
                                          ins=[v_in[tau].ap().opt()],
                                          outs=[v_att[tau].ap().opt()]),
                    "sCC")
            ctx.fence()

        p3stack = ExitStack()
        sow_all = p3stack.enter_context(nc.sbuf_tensor("b_swall", [128, FC * 2048], BF16))
        wof = [p3stack.enter_context(nc.sbuf_tensor(f"b_wf{i}", [128, 2048], F32))
               for i in range(2)]
        sow_state = {"done": 0}

        qkA = [t.ap() for t in qk_att]
        vA = [t.ap() for t in v_att]
        oI = [t.ap() for t in o_in]
        with ExitStack() as ast:
            ae = ast.enter_context
            sqt = [ae(nc.sbuf_tensor(f"a_sq{i}", [128, 512], BF16)) for i in range(2)]
            skb = [ae(nc.sbuf_tensor(f"a_sk{i}", [128, 512], BF16)) for i in range(4)]
            svk = [ae(nc.sbuf_tensor(f"a_sv{i}", [128, 4 * 128], BF16)) for i in range(4)]
            PTa = ae(nc.sbuf_tensor("a_pt", [128, 4096], BF16))
            psSa = ae(nc.psum_tensor("a_psS", [128, 2048], F32))
            poT = ae(nc.psum_tensor("a_poT", [128, 512], F32))
            psL = ae(nc.psum_tensor("a_psL", [1, 512], F32))
            lbc = ae(nc.psum_tensor("a_lbc", [128, 512], F32))
            poTs = ae(nc.sbuf_tensor("a_poTs", [128, 512], F32))
            lrow = ae(nc.sbuf_tensor("a_lrow", [1, 512], F32))
            linvb = ae(nc.sbuf_tensor("a_linv", [128, 512], F32))
            osb = ae(nc.sbuf_tensor("a_osb", [128, 512], BF16))

            exp_last = {0: 0, 1: 0}
            avq_last = {0: 0, 1: 0, 2: 0, 3: 0}
            s_hist, av_hist, q_hist = {}, {}, {}
            state = {"evac": 0, "norm": 0, "ost": 0}
            tail = [None]
            pend = []

            def post_av(g):
                kb_, p_, qt_, gk_, hp, aw, _tl = g
                if not (kb_ == qt_ and p_ == 1):
                    return
                eg = ctx.dinc(pe, "sPE")
                wt(dve, "DVE", "sPE", eg)
                dve.tensor_copy(poTs[:], poT[:])
                dve.tensor_copy(lrow[:], psL[:])
                state["evac"] = ctx.dinc(dve, "sDVE")
                tail[0] = _tl

            def emit_tail():
                if tail[0] is None:
                    return
                rq_, hl_, qbase_ = tail[0]
                wt(pe, "PE", "sDVE", state["evac"])
                wt(pe, "PE", "sDVE", state["norm"])
                pe.matmul(lbc[:], ones1[:], lrow[:], start=True, stop=True)
                vB = ctx.dinc(pe, "sPE")
                wt(dve, "DVE", "sPE", vB)
                wt(dve, "DVE", "qG", state["ost"])
                dve.reciprocal(linvb[:], lbc[:])
                dve.tensor_tensor(osb[:], poTs[:], linvb[:], ALU.mult)
                state["norm"] = ctx.dinc(dve, "sDVE")
                wt(gp, "GP", "sDVE", state["norm"])
                orow = rq_ * 128
                state["ost"] = st(oI[hl_][orow:orow + 128, qbase_:qbase_ + 512], osb[:])
                tail[0] = None

            def emit_av(g):
                kb_, p_, qt_, gk_, hp, aw, _tl = g
                if kb_ == 0 and p_ == 0:
                    emit_tail()
                    wt(pe, "PE", "sDVE", state["evac"])
                wt(pe, "PE", aw["vs"], aw["vv"])
                wt(pe, "PE", aw["w"][0], aw["w"][1])
                svr_ = svk[gk_ % 4][:].rearrange("p (s n) -> p s n", s=4)
                pcol = hp * 1024
                mm = None
                for si in range(2):
                    s = 2 * p_ + si
                    pe.matmul(psL[:], onesb[:],
                              PTa[:, pcol + si * 512:pcol + (si + 1) * 512],
                              start=(kb_ == 0 and s == 0),
                              stop=(kb_ == qt_ and s == 3))
                for si in range(2):
                    s = 2 * p_ + si
                    mm = pe.matmul(poT[:], svr_[:, s],
                                   PTa[:, pcol + si * 512:pcol + (si + 1) * 512],
                                   start=(kb_ == 0 and s == 0),
                                   stop=(kb_ == qt_ and s == 3))
                avv = ctx.inc(mm, "sPE")
                avq_last[hp] = avv
                av_hist[gk_] = avv
                return avv

            it = 0
            gk = 0
            gq = 0
            q_last_S = {}
            cc_oh = {}
            for hl in range(2):
                for b in range(2):
                    for qt in range(16):
                        jq, qbase = key_loc(qt)
                        rq = 4 * b + jq
                        wt(syn, "SY", "sCC", cc_qkv)
                        if it >= 2:
                            wt(syn, "SY", "sPE", q_last_S.get(it - 2, 0))
                        q_hist[it] = ctx.lds(
                            f"sq{it % 2}", sqt[it % 2][:],
                            qkA[qbase // 1024][rq * 512 + hl * 128:
                                               rq * 512 + hl * 128 + 128,
                                qbase % 1024:qbase % 1024 + 512])
                        for kb in range(qt + 1):
                            jk, kbase = key_loc(kb)
                            rk = 4 * b + jk
                            krow = rk * 512 + 256 + hl * 128
                            ktau, kloc = kbase // 1024, kbase % 1024
                            vrow = rk * 1024 + kloc
                            if gk >= 4:
                                wt(syn, "SY", "sPE", s_hist.get(gk - 4, 0))
                                wt(syn, "SY", "sPE", av_hist.get(gk - 4, 0))
                            ctx.lds(f"kv{gk % 4}", skb[gk % 4][:],
                                    qkA[ktau][krow:krow + 128, kloc:kloc + 512])
                            vv = ctx.lds(f"kv{gk % 4}",
                                         svk[gk % 4][:].rearrange("p (s n) -> p s n", s=4),
                                         vA[ktau][vrow:vrow + 512, hl * 128:(hl + 1) * 128]
                                         .rearrange("(s p) n -> p s n", s=4, p=128))
                            wt(pe, "PE", f"kv{gk % 4}", vv)
                            wt(pe, "PE", f"sq{it % 2}", q_hist[it])
                            for p in range(2):
                                wt(pe, "PE", "sACT", exp_last[p])
                                for si in range(2):
                                    s = 2 * p + si
                                    pe.matmul(psSa[:, s * 512:(s + 1) * 512],
                                              skb[gk % 4][:][:, s * 128:(s + 1) * 128],
                                              sqt[it % 2][:], start=True, stop=True)
                            vS = ctx.dinc(pe, "sPE")
                            s_hist[gk] = vS
                            q_last_S[it] = vS
                            for p in range(2):
                                hp = gq % 4
                                gq += 1
                                pcol = hp * 1024
                                wt(act, "ACT", "sPE", vS)
                                wt(act, "ACT", "sPE", avq_last[hp])
                                act.activation(PTa[:, pcol:pcol + 1024],
                                               psSa[:, p * 1024:(p + 1) * 1024],
                                               AF.Exp, bias=shiftc[:], scale=1.0)
                                exp_last[p] = ctx.dinc(act, "sACT")
                                aw = {"vv": vv, "vs": f"kv{gk % 4}",
                                      "w": ("sACT", exp_last[p])}
                                if kb == qt:
                                    wt(dve, "DVE", "sACT", exp_last[p])
                                    dve.tensor_tensor(
                                        PTa[:, pcol:pcol + 1024],
                                        PTa[:, pcol:pcol + 1024],
                                        c_masks[:, p * 1024:(p + 1) * 1024], ALU.mult)
                                    aw = {"vv": vv, "vs": f"kv{gk % 4}",
                                          "w": ("sDVE", ctx.dinc(dve, "sDVE"))}
                                pend.append((kb, p, qt, gk, hp, aw, (rq, hl, qbase)))
                                if len(pend) > 2:
                                    g = pend.pop(0)
                                    emit_av(g)
                                    post_av(g)
                            gk += 1
                        it += 1
                if hl == 0:
                    while pend:
                        g = pend.pop(0)
                        emit_av(g)
                        post_av(g)
                    emit_tail()
                    socv = {}
                    for hd in range(16):
                        if hd >= 2:
                            wt(syn, "SY", "sDVE", socv[hd - 2])
                        vf = ctx.lds(f"wof{hd % 2}", wof[hd % 2][:], owfr[hd])
                        wt(dve, "DVE", f"wof{hd % 2}", vf)
                        dve.tensor_copy(sow_all[:, hd * 2048:(hd + 1) * 2048],
                                        wof[hd % 2][:])
                        socv[hd] = ctx.dinc(dve, "sDVE")
                    sow_state["done"] = socv[15]
                    wt(gp, "GP", "qG", cnt["qG"])
                    with allow_ici_all_to_all():
                        cc_oh[0] = ctx.inc(
                            gp.collective_compute("AllToAll", ALU.bypass,
                                                  replica_groups=[list(range(R))],
                                                  ins=[o_in[0].ap().opt()],
                                                  outs=[o_att[0].ap().opt()]),
                            "sCC")
            while pend:
                g = pend.pop(0)
                emit_av(g)
                post_av(g)
            emit_tail()

        wt(gp, "GP", "qG", cnt["qG"])
        with allow_ici_all_to_all():
            cc_o = ctx.inc(
                gp.collective_compute("AllToAll", ALU.bypass,
                                      replica_groups=[list(range(R))],
                                      ins=[o_in[1].ap().opt()], outs=[o_att[1].ap().opt()]),
                "sCC")

        oA = [t.ap() for t in o_att]
        att_pe_final = cnt["sPE"]
        att_dve_final = cnt["sDVE"]
        sow_done = sow_state["done"]
        cc_o0 = cc_oh[0]
        with ExitStack() as bst:
            be = bst.enter_context
            ofm = [be(nc.sbuf_tensor(f"b_of{i}", [128, FC * 512], BF16)) for i in range(2)]
            x1c = [be(nc.sbuf_tensor(f"b_x1{i}", [128, 512], F32)) for i in range(2)]
            x2c = [be(nc.sbuf_tensor(f"b_x2{i}", [128, 512], F32)) for i in range(2)]
            wt(syn, "SY", "sPE", att_pe_final)
            wt(syn, "SY", "sDVE", att_dve_final)
            with nc.psum_tensor("b_j0", [128, 512], F32) as bj0, \
                 nc.psum_tensor("b_j1", [128, 512], F32) as bj1:
                pj = [bj0, bj1]
                op_pe, op_dve, op_st = {}, {}, {}
                qb_last_pe, vofm = {}, {}
                for qb in range(4):
                    q0 = qb * 512
                    if qb >= 2:
                        wt(syn, "SY", "sPE", qb_last_pe[qb - 2])
                    wt(syn, "SY", "sCC", cc_o0)
                    for fc in range(0, 16, 2):
                        ctx.lds(f"of{qb % 2}",
                                ofm[qb % 2][:, fc * 512:(fc + 1) * 512],
                                oA[0][(fc // 2) * 128:(fc // 2) * 128 + 128,
                                      q0:q0 + 512])
                    wt(syn, "SY", "sCC", cc_o)
                    v = 0
                    for fc in range(1, 16, 2):
                        v = ctx.lds(f"of{qb % 2}",
                                    ofm[qb % 2][:, fc * 512:(fc + 1) * 512],
                                    oA[1][(fc // 2) * 128:(fc // 2) * 128 + 128,
                                          q0:q0 + 512])
                    vofm[qb] = v
                    for mc in range(16):
                        gi = qb * 16 + mc
                        if gi >= 2:
                            wt(syn, "SY", "sPE", op_pe[gi - 2])
                            wt(syn, "SY", "sDVE", op_dve[gi - 2])
                        vx1 = ctx.lds(f"so{gi % 2}", x1c[gi % 2][:],
                                      x1r[mc][:, q0:q0 + 512])
                        wt(pe, "PE", "sDVE", sow_done)
                        wt(pe, "PE", f"of{qb % 2}", vofm[qb])
                        if gi == 0:
                            wt(pe, "PE", "sPE", att_pe_final)
                            wt(pe, "PE", "sDVE", att_dve_final)
                        if gi >= 2:
                            wt(pe, "PE", "sDVE", op_dve[gi - 2])
                        for hd in range(16):
                            pe.matmul(pj[gi % 2][:],
                                      sow_all[:, hd * 2048 + mc * 128:
                                              hd * 2048 + (mc + 1) * 128],
                                      ofm[qb % 2][:, hd * 512:(hd + 1) * 512],
                                      start=(hd == 0), stop=(hd == 15))
                        op_pe[gi] = ctx.dinc(pe, "sPE")
                        wt(dve, "DVE", "sPE", op_pe[gi])
                        wt(dve, "DVE", f"so{gi % 2}", vx1)
                        if gi >= 2:
                            wt(dve, "DVE", "qG", op_st[gi - 2])
                        dve.scalar_tensor_tensor(x2c[gi % 2][:], pj[gi % 2][:],
                                                 c_outbT[:, mc:mc + 1], x1c[gi % 2][:],
                                                 ALU.add, ALU.add)
                        op_dve[gi] = ctx.dinc(dve, "sDVE")
                        wt(gp, "GP", "sDVE", op_dve[gi])
                        op_st[gi] = st(x2r[mc][:, q0:q0 + 512], x2c[gi % 2][:])
                    qb_last_pe[qb] = op_pe[qb * 16 + 15]
        p3stack.close()
        x2_qG = cnt["qG"]
        ctx.fence()

        emit_ffn(x2r, x2r, outr, "f2", cc_gu=cc_u, cc_dn=cc_d, dep_qG=x2_qG)

        if debug:
            dbg_pairs = [("dx1", x1D), ("dx2", x2D), ("dwg", wgD)]
            ctx.fence()
            for nm, ten in dbg_pairs:
                dd = dt_(nm, list(ten.shape), ten.dtype, kind="ExternalOutput")
                st(dd.ap(), ten.ap())

        wt(gp, "GP", "qG", cnt["qG"])
        wt(syn, "SY", "qG", cnt["qG"])
        wt(syn, "SY", "qS", cnt["qS"])
    return nc


def _host_prepare(inputs):
    x = np.asarray(inputs["L0_attn_x"], dtype=np.float32)
    qkv_w = np.asarray(inputs["L0_attn_qkv_w"], dtype=np.float32)
    qkv_b = np.asarray(inputs["L0_attn_qkv_b"], dtype=np.float32)
    out_w = np.asarray(inputs["L0_attn_out_w"], dtype=np.float32)
    out_b = np.asarray(inputs["L0_attn_out_b"], dtype=np.float32)
    lut = np.asarray(inputs["L0_ffn_lut"], dtype=np.float32)
    g_idx = np.asarray(inputs["L0_ffn_gate_idx"], dtype=np.int32)
    u_idx = np.asarray(inputs["L0_ffn_up_idx"], dtype=np.int32)
    d_idx = np.asarray(inputs["L0_ffn_down_idx"], dtype=np.int32)

    qkvwT = np.ascontiguousarray(qkv_w.T)
    qkvbT = np.ascontiguousarray(qkv_b.reshape(48, 128).T)
    vbias = np.ascontiguousarray(qkv_b[4096:].reshape(1, D))
    outwT = np.ascontiguousarray(out_w.T)
    outbT = np.ascontiguousarray(out_b.reshape(16, 128).T)
    lut2 = np.ascontiguousarray(lut.reshape(1, LUT))
    gT = np.ascontiguousarray(g_idx.T)
    uT = np.ascontiguousarray(u_idx.T)
    dT = np.ascontiguousarray(d_idx.T)

    freqs = np.arange(0, HD, 2, dtype=np.float32) / HD
    pos = np.arange(SEQ, dtype=np.float32)
    ang = pos[:, None] * freqs[None, :]
    emb = np.concatenate([np.sin(ang), np.cos(ang)], axis=-1)
    embT = np.ascontiguousarray(emb.T.astype(np.float32))

    kk = np.arange(128)[:, None]
    qq = np.arange(512)[None, :]
    masks = np.concatenate(
        [(128 * m + kk <= qq) for m in range(4)], axis=1).astype(ml_dtypes.bfloat16)

    in_maps, metas = [], []
    for cid in range(R):
        bb = cid // 4
        jj = cid % 4
        tok = np.concatenate([np.arange(ca * 1024, (ca + 1) * 1024) for ca in (jj, 7 - jj)])
        im = {
            "xT": np.ascontiguousarray(x[bb][tok].T),
            "qkvwT": qkvwT, "qkvbT": qkvbT, "vbias": vbias,
            "outwT": outwT, "outbT": outbT, "lut": lut2,
            "gidx": np.ascontiguousarray(gT[cid * 256:(cid + 1) * 256]),
            "uidx": np.ascontiguousarray(uT[cid * 256:(cid + 1) * 256]),
            "didx": np.ascontiguousarray(dT[:, cid * 256:(cid + 1) * 256]),
            "embq": np.ascontiguousarray(embT[:, tok] * np.float32(1.0 / np.sqrt(HD))),
            "embk": np.ascontiguousarray(embT[:, tok]),
            "masks": masks,
        }
        in_maps.append(im)
        metas.append((bb, tok))
    return in_maps, metas


def kernel(**inputs):
    in_maps, metas = _host_prepare(inputs)
    nc = build()
    res = run_bass_kernel_spmd(nc, in_maps, core_ids=list(range(R)))
    out = np.empty((B, SEQ, D), dtype=np.float32)
    for cid in range(R):
        bb, tok = metas[cid]
        out[bb, tok, :] = np.asarray(res.results[cid]["out"]).T
    return out


# version 48
# speedup vs baseline: 1.0061x; 1.0037x over previous
import numpy as np
import ml_dtypes
import concourse.bass as bass
import concourse.mybir as mybir
from concourse.bass_utils import run_bass_kernel_spmd
from concourse.aot_env import allow_ici_all_to_all
from contextlib import ExitStack

F32 = mybir.dt.float32
BF16 = mybir.dt.bfloat16
I32 = mybir.dt.int32
AF = mybir.ActivationFunctionType
ALU = mybir.AluOpType

R = 8
B, SEQ, D = 2, 8192, 2048
H, HD = 16, 128
T = 2048
FC = 16
INTER = 5504
OC = 43
LUT = 16
EXP_SHIFT = 8.0
INV_D = 1.0 / 2048.0


class Ctx:
    def __init__(self, nc, stack):
        self.nc = nc
        self.stack = stack
        self.c = {}
        self.sem = {}
        self.last_wait = {}
        for name in ("qS", "qG", "sCC", "sPE", "sACT", "sDVE"):
            self.sem[name] = stack.enter_context(nc.semaphore(name))
            self.c[name] = 0

    def _sem(self, name):
        if name not in self.sem:
            self.sem[name] = self.stack.enter_context(self.nc.semaphore(name))
            self.c[name] = 0
        return self.sem[name]

    def lds(self, sname, dst, src):
        s = self._sem(sname)
        instr = self.nc.sync.dma_start(dst, src)
        instr.then_inc(s, 16)
        self.c[sname] += 16
        return self.c[sname]

    def lda(self, sname, dst, src):
        s = self._sem(sname)
        instr = self.nc.scalar.dma_start(dst, src)
        instr.then_inc(s, 16)
        self.c[sname] += 16
        return self.c[sname]

    def inc(self, instr, name, n=1):
        instr.then_inc(self.sem[name], n)
        self.c[name] += n
        return self.c[name]

    def dinc(self, eng, name, n=1):
        return self.inc(eng.drain(), name, n)

    def ld(self, dst, src):
        return self.inc(self.nc.sync.dma_start(dst, src), "qS", 16)

    def st(self, dst, src):
        return self.inc(self.nc.gpsimd.dma_start(dst, src), "qG", 16)

    def wt(self, eng, ename, sname, v):
        if v is None or v <= 0:
            return
        self._sem(sname)
        if sname in ("qS", "qG"):
            v = self.c[sname]
        key = (ename, sname)
        if self.last_wait.get(key, 0) >= v:
            return
        eng.wait_ge(self.sem[sname], v)
        self.last_wait[key] = v

    def fence(self):
        nc = self.nc
        engs = [(nc.sync, "SY"), (nc.gpsimd, "GP"), (nc.tensor, "PE"),
                (nc.scalar, "ACT"), (nc.vector, "DVE")]
        own = {"PE": "sPE", "ACT": "sACT", "DVE": "sDVE"}
        snap = dict(self.c)
        for eng, en in engs:
            for sname in ("qS", "qG", "sPE", "sACT", "sDVE"):
                if own.get(en) == sname:
                    continue
                self.wt(eng, en, sname, snap[sname])


def key_loc(kb):
    kc = kb // 2
    jk = min(kc, 7 - kc)
    base = (0 if kc <= 3 else 1024) + (kb % 2) * 512
    return jk, base


def build(debug=False):
    nc = bass.Bass("TRN2", target_bir_lowering=False)
    dt_ = nc.dram_tensor
    xT = dt_("xT", [D, T], F32, kind="ExternalInput").ap()
    qkvwT = dt_("qkvwT", [D, 3 * D], F32, kind="ExternalInput").ap()
    qkvbT = dt_("qkvbT", [128, 48], F32, kind="ExternalInput").ap()
    vbiasI = dt_("vbias", [1, D], F32, kind="ExternalInput").ap()
    outwT = dt_("outwT", [D, D], F32, kind="ExternalInput").ap()
    outbT = dt_("outbT", [128, 16], F32, kind="ExternalInput").ap()
    lutI = dt_("lut", [1, LUT], F32, kind="ExternalInput").ap()
    gidx = dt_("gidx", [256, INTER], I32, kind="ExternalInput").ap()
    uidx = dt_("uidx", [256, INTER], I32, kind="ExternalInput").ap()
    didx = dt_("didx", [INTER, 256], I32, kind="ExternalInput").ap()
    embqI = dt_("embq", [128, T], F32, kind="ExternalInput").ap()
    embkI = dt_("embk", [128, T], F32, kind="ExternalInput").ap()
    masksI = dt_("masks", [128, 4 * 512], BF16, kind="ExternalInput").ap()
    outD = dt_("out", [D, T], F32, kind="ExternalOutput").ap()

    wg_sh = dt_("wg_sh", [2 * OC * 128, 128], BF16)
    wu_sh = dt_("wu_sh", [2 * OC * 128, 128], BF16)
    wd_sh = dt_("wd_sh", [2 * OC * 128, 128], BF16)
    wgD = dt_("wg", [16 * OC * 128, 128], BF16)
    wuD = dt_("wu", [16 * OC * 128, 128], BF16)
    wdD = dt_("wd", [16 * OC * 128, 128], BF16)
    qk_in = [dt_(f"qk_in{i}", [R * 512, 1024], BF16) for i in range(2)]
    qk_att = [dt_(f"qk_att{i}", [R * 512, 1024], BF16) for i in range(2)]
    v_in = [dt_(f"v_in{i}", [R * 1024, 256], BF16) for i in range(2)]
    v_att = [dt_(f"v_att{i}", [R * 1024, 256], BF16) for i in range(2)]
    o_in = [dt_(f"o_in{i}", [R * 128, T], BF16) for i in range(2)]
    o_att = [dt_(f"o_att{i}", [R * 128, T], BF16) for i in range(2)]
    x1D = dt_("x1", [D, T], F32)
    x2D = dt_("x2", [D, T], F32)

    with ExitStack() as stack:
        e = stack.enter_context
        ctx = Ctx(nc, stack)
        cnt = ctx.c
        wt, ld, st = ctx.wt, ctx.ld, ctx.st
        syn, gp, pe, act, dve = nc.sync, nc.gpsimd, nc.tensor, nc.scalar, nc.vector

        c_masks = e(nc.sbuf_tensor("c_masks", [128, 4 * 512], BF16))
        c_qkvbT = e(nc.sbuf_tensor("c_qkvbT", [128, 48], F32))
        c_outbT = e(nc.sbuf_tensor("c_outbT", [128, 16], F32))
        c_lut = e(nc.sbuf_tensor("c_lut", [1, LUT], F32))
        c_vbias = e(nc.sbuf_tensor("c_vbias", [1, D], F32))
        ones1 = e(nc.sbuf_tensor("ones1", [1, 128], F32))
        onesb = e(nc.sbuf_tensor("onesb", [128, 1], BF16))
        lutbc = e(nc.sbuf_tensor("lutbc", [128, LUT], F32))
        lutbch = e(nc.sbuf_tensor("lutbch", [128, LUT], F32))
        vbias_bc = e(nc.sbuf_tensor("vbias_bc", [128, D], BF16))
        epsc = e(nc.sbuf_tensor("epsc", [1, 1], F32))
        shiftc = e(nc.sbuf_tensor("shiftc", [128, 1], F32))

        ctx.lds("cst", c_masks[:], masksI)
        ctx.lds("cst", c_qkvbT[:], qkvbT)
        ctx.lds("cst", c_outbT[:], outbT)
        ctx.lds("cst", c_lut[:], lutI)
        consts_loaded = ctx.lds("cst", c_vbias[:], vbiasI)

        dve.memset(ones1[:], 1.0)
        dve.memset(onesb[:], 1.0)
        dve.memset(epsc[:], 1e-5)
        with nc.psum_tensor("ps_lbc", [128, LUT], F32) as ps_lbc, \
             nc.psum_tensor("ps_vb", [128, D], F32) as ps_vb:
            dve.memset(shiftc[:], -EXP_SHIFT)
            msets = ctx.dinc(dve, "sDVE")
            wt(pe, "PE", "cst", consts_loaded)
            wt(pe, "PE", "sDVE", msets)
            pe.matmul(ps_lbc[:], ones1[:], c_lut[:], start=True, stop=True)
            for i in range(4):
                pe.matmul(ps_vb[:, i * 512:(i + 1) * 512], ones1[:],
                          c_vbias[:, i * 512:(i + 1) * 512], start=True, stop=True)
            v0 = ctx.dinc(pe, "sPE")
            wt(dve, "DVE", "sPE", v0)
            dve.tensor_copy(lutbc[:], ps_lbc[:])
            dve.tensor_scalar(lutbch[:], ps_lbc[:], 0.5, None, ALU.mult)
            dve.tensor_copy(vbias_bc[:], ps_vb[:])
            consts_done = ctx.dinc(dve, "sDVE")

        with nc.sbuf_tensor("m_idx", [128, 2 * INTER], I32) as m_idx, \
             nc.sbuf_tensor("m_idxb", [128, 2 * INTER], BF16) as m_idxb, \
             nc.sbuf_tensor("m_acc", [128, 2 * INTER], BF16) as m_acc, \
             nc.sbuf_tensor("m_term", [128, 2 * INTER], BF16) as m_term:

            mat_prev = {"vd": 0, "stores": 0}

            def materialize(kind, idx_ap, sh, full, lcol):
                wt(syn, "SY", "sDVE", mat_prev["vd"])
                if kind == "g":
                    vl = ctx.lds("mix", m_idx[:].rearrange("p (l o) -> p l o", l=2),
                                 idx_ap.rearrange("(l p) o -> p l o", l=2, p=128))
                else:
                    vl = ctx.lds("mix", m_idx[:].rearrange("p (i m) -> p i m", i=OC),
                                 idx_ap.rearrange("(i p) m -> p i m", i=OC, p=128))
                wt(dve, "DVE", "mix", vl)
                wt(dve, "DVE", "sDVE", consts_done)
                wt(dve, "DVE", "qG", mat_prev["stores"])
                dve.tensor_copy(m_idxb[:], m_idx[:])
                for v in range(LUT):
                    dve.tensor_scalar(m_term[:], m_idxb[:], float(v),
                                      lcol[:, v:v + 1], ALU.is_equal, ALU.mult)
                    if v == 0:
                        dve.tensor_copy(m_acc[:], m_term[:])
                    else:
                        dve.tensor_tensor(m_acc[:], m_acc[:], m_term[:], ALU.add)
                vd = ctx.dinc(dve, "sDVE")
                mat_prev["vd"] = vd
                wt(gp, "GP", "sDVE", vd)
                shr = sh.ap().rearrange("(l o p) n -> l p o n", l=2, o=OC, p=128)
                if kind == "g":
                    srcr = m_acc[:].rearrange("p (l o n) -> p l o n", l=2, o=OC, n=128)
                    srcs = [srcr[:, 0], srcr[:, 1]]
                else:
                    srcr = m_acc[:].rearrange("p (i l n) -> p i l n", i=OC, l=2, n=128)
                    srcs = [srcr[:, :, 0], srcr[:, :, 1]]
                for blk in range(2):
                    st(shr[blk], srcs[blk])
                mat_prev["stores"] = cnt["qG"]
                wt(gp, "GP", "qG", cnt["qG"])
                return ctx.inc(
                    gp.collective_compute("AllGather", ALU.bypass,
                                          replica_groups=[list(range(R))],
                                          ins=[sh.ap().opt()], outs=[full.ap().opt()]),
                    "sCC")

            cc_g = materialize("g", gidx, wg_sh, wgD, lutbc)
            cc_u = materialize("g", uidx, wu_sh, wuD, lutbc)
            cc_d = materialize("d", didx, wd_sh, wdD, lutbch)

        ctx.fence()


        def emit_ln(src_r, xn, tau, tag, dep_qG=0):
            v_norm = 0
            with ExitStack() as lst:
                le = lst.enter_context
                xb = le(nc.sbuf_tensor(f"lx_{tag}", [128, FC * 512], BF16))
                xl = [le(nc.sbuf_tensor(f"ll{i}_{tag}", [128, 512], F32)) for i in range(6)]
                sq2 = [le(nc.sbuf_tensor(f"lq{i}_{tag}", [128, 512], BF16)) for i in range(2)]
                sts = le(nc.sbuf_tensor(f"ls_{tag}", [1, 512 * 6], F32))
                bcb = le(nc.sbuf_tensor(f"lb_{tag}", [128, 1024], BF16))
                mean, ex2 = sts[:, 0:512], sts[:, 512:1024]
                var, lnv = sts[:, 1024:1536], sts[:, 1536:2048]
                rr, mr = sts[:, 2048:2560], sts[:, 2560:3072]
                prev_pe = 0
                prev_norm = 0
                prev_cv_last = 0
                g_norm = [0, 0]
                for g in range(2):
                    t0 = tau * 1024 + g * 512
                    with nc.psum_tensor(f"lps_{tag}{g}", [1, 512], F32) as ps_su, \
                         nc.psum_tensor(f"lpq_{tag}{g}", [1, 512], F32) as ps_sq, \
                         nc.psum_tensor(f"lpb_{tag}{g}", [128, 1024], F32) as ps_b:
                        cvs, sqv, smm = {}, {}, {}
                        wt(syn, "SY", "qG", dep_qG)
                        for kc in range(FC):
                            if kc >= 6:
                                wt(syn, "SY", "sDVE", cvs[kc - 6])
                            else:
                                wt(syn, "SY", "sDVE", prev_cv_last)
                            vl = ctx.lds(f"xl{kc % 6}", xl[kc % 6][:],
                                         src_r[kc][:, t0:t0 + 512])
                            wt(dve, "DVE", f"xl{kc % 6}", vl)
                            if kc == 0:
                                wt(dve, "DVE", "sPE", prev_pe)
                            dve.tensor_copy(xb[:, kc * 512:(kc + 1) * 512], xl[kc % 6][:])
                            cvs[kc] = ctx.dinc(dve, "sDVE")
                            wt(act, "ACT", "sDVE", cvs[kc])
                            if kc == 0:
                                wt(act, "ACT", "sPE", prev_pe)
                            if kc >= 2:
                                wt(act, "ACT", "sPE", smm[kc - 2])
                            act.square(sq2[kc % 2][:],
                                       xb[:, kc * 512:(kc + 1) * 512])
                            sqv[kc] = ctx.dinc(act, "sACT")
                            wt(pe, "PE", "sACT", sqv[kc])
                            pe.matmul(ps_su[:], onesb[:], xb[:, kc * 512:(kc + 1) * 512],
                                      start=(kc == 0), stop=(kc == FC - 1))
                            mmq = pe.matmul(ps_sq[:], onesb[:], sq2[kc % 2][:],
                                            start=(kc == 0), stop=(kc == FC - 1))
                            smm[kc] = ctx.inc(mmq, "sPE")
                        prev_cv_last = cvs[FC - 1]
                        prev_pe = ctx.dinc(pe, "sPE")
                        wt(dve, "DVE", "sPE", prev_pe)
                        dve.tensor_scalar(mean, ps_su[:], INV_D, None, ALU.mult)
                        dve.tensor_scalar(ex2, ps_sq[:], INV_D, None, ALU.mult)
                        dve.tensor_tensor(var, mean, mean, ALU.mult)
                        dve.tensor_tensor(var, ex2, var, ALU.subtract)
                        v_var = ctx.dinc(dve, "sDVE")
                        wt(act, "ACT", "sDVE", v_var)
                        act.activation(lnv, var, AF.Ln, bias=epsc[:], scale=1.0)
                        act.activation(rr, lnv, AF.Exp, scale=-0.5)
                        v_r = ctx.dinc(act, "sACT")
                        wt(dve, "DVE", "sACT", v_r)
                        dve.tensor_tensor(mr, mean, rr, ALU.mult)
                        v_mr = ctx.dinc(dve, "sDVE")
                        wt(pe, "PE", "sDVE", v_mr)
                        wt(pe, "PE", "sDVE", prev_norm)
                        pe.matmul(ps_b[:, 0:512], ones1[:], rr, start=True, stop=True)
                        pe.matmul(ps_b[:, 512:1024], ones1[:], mr, start=True, stop=True)
                        v_bc = ctx.dinc(pe, "sPE")
                        wt(dve, "DVE", "sPE", v_bc)
                        dve.tensor_copy(bcb[:], ps_b[:])
                        for kc in range(FC):
                            col = kc * 1024 + g * 512
                            dve.tensor_tensor(xn[:, col:col + 512],
                                              xb[:, kc * 512:(kc + 1) * 512],
                                              bcb[:, 0:512], ALU.mult)
                            dve.tensor_tensor(xn[:, col:col + 512],
                                              xn[:, col:col + 512],
                                              bcb[:, 512:1024], ALU.subtract)
                        prev_norm = v_norm = ctx.dinc(dve, "sDVE")
                        g_norm[g] = v_norm
            return g_norm

        wgR = wgD.ap().rearrange("(kc o p) n -> o p kc n", kc=16, o=OC, p=128)
        wuR = wuD.ap().rearrange("(kc o p) n -> o p kc n", kc=16, o=OC, p=128)
        wdR = wdD.ap().rearrange("(mc i p) n -> mc p i n", mc=16, i=OC, p=128)
        xTr = xT.rearrange("(kc p) t -> kc p t", kc=FC, p=128)
        x1r = x1D.ap().rearrange("(kc p) t -> kc p t", kc=FC, p=128)
        x2r = x2D.ap().rearrange("(kc p) t -> kc p t", kc=FC, p=128)
        outr = outD.rearrange("(kc p) t -> kc p t", kc=FC, p=128)
        qwfR = qkvwT.rearrange("(kc p) c -> p kc c", kc=FC, p=128)
        owfr = outwT.rearrange("(kc p) c -> kc p c", kc=FC, p=128)

        def emit_ffn(src_r, res_r, dst_r, tag, cc_gu, cc_dn, dep_qG=0):
            for tau in range(2):
                ts = tau * 1024
                with ExitStack() as fst:
                    fe = fst.enter_context
                    xn = fe(nc.sbuf_tensor(f"xn_{tag}{tau}", [128, FC * 1024], BF16))
                    wbg = [fe(nc.sbuf_tensor(f"wbg{i}_{tag}{tau}", [128, 2048], BF16)) for i in range(3)]
                    wbu = [fe(nc.sbuf_tensor(f"wbu{i}_{tag}{tau}", [128, 2048], BF16)) for i in range(3)]
                    hsil = [fe(nc.sbuf_tensor(f"hs{i}_{tag}{tau}", [128, 1024], BF16)) for i in range(2)]
                    hbuf = fe(nc.sbuf_tensor(f"h_{tag}{tau}", [128, OC * 1024], BF16))
                    v_xng = emit_ln(src_r, xn, tau, f"{tag}{tau}", dep_qG=dep_qG)
                    v_xn = v_xng[1]
                    if True:
                        pe_done, silu_done, hm_done = {}, {}, {}
                        with nc.psum_tensor(f"pg0_{tag}{tau}", [128, 1024], F32) as pg0, \
                             nc.psum_tensor(f"pg1_{tag}{tau}", [128, 1024], F32) as pg1, \
                             nc.psum_tensor(f"pu0_{tag}{tau}", [128, 1024], F32) as pu0, \
                             nc.psum_tensor(f"pu1_{tag}{tau}", [128, 1024], F32) as pu1:
                            pg, pu = [pg0, pg1], [pu0, pu1]
                            for oc in range(OC):
                                if oc >= 3:
                                    wt(syn, "SY", "sPE", pe_done[oc - 3])
                                else:
                                    wt(syn, "SY", "sCC", cc_gu)
                                ctx.lds(f"gu{oc % 3}",
                                        wbg[oc % 3][:].rearrange("p (kc n) -> p kc n", kc=16),
                                        wgR[oc])
                                vlu = ctx.lds(f"gu{oc % 3}",
                                              wbu[oc % 3][:].rearrange("p (kc n) -> p kc n", kc=16),
                                              wuR[oc])
                                wt(pe, "PE", f"gu{oc % 3}", vlu)
                                if oc >= 2:
                                    wt(pe, "PE", "sACT", silu_done[oc - 2])
                                    wt(pe, "PE", "sDVE", hm_done[oc - 2])
                                for hf in range(2):
                                    wt(pe, "PE", "sDVE", v_xng[hf])
                                    for buf, ps in ((wbg, pg), (wbu, pu)):
                                        for kc in range(FC):
                                            pe.matmul(
                                                ps[oc % 2][:, hf * 512:(hf + 1) * 512],
                                                buf[oc % 3][:, kc * 128:(kc + 1) * 128],
                                                xn[:, kc * 1024 + hf * 512:
                                                   kc * 1024 + (hf + 1) * 512],
                                                start=(kc == 0), stop=(kc == FC - 1))
                                pe_done[oc] = ctx.dinc(pe, "sPE")
                                wt(act, "ACT", "sPE", pe_done[oc])
                                if oc >= 2:
                                    wt(act, "ACT", "sDVE", hm_done[oc - 2])
                                act.activation(hsil[oc % 2][:], pg[oc % 2][:], AF.Silu)
                                silu_done[oc] = ctx.dinc(act, "sACT")
                                wt(dve, "DVE", "sACT", silu_done[oc])
                                dve.tensor_tensor(hbuf[:, oc * 1024:(oc + 1) * 1024],
                                                  pu[oc % 2][:], hsil[oc % 2][:], ALU.mult)
                                hm_done[oc] = ctx.dinc(dve, "sDVE")
                        with nc.sbuf_tensor(f"wdh0_{tag}{tau}", [128, 22 * 128], BF16) as wdh0, \
                             nc.sbuf_tensor(f"wdh1_{tag}{tau}", [128, 22 * 128], BF16) as wdh1, \
                             nc.sbuf_tensor(f"wdh2_{tag}{tau}", [128, 22 * 128], BF16) as wdh2, \
                             nc.sbuf_tensor(f"xr0_{tag}{tau}", [128, 1024], F32) as xr0, \
                             nc.sbuf_tensor(f"xr1_{tag}{tau}", [128, 1024], F32) as xr1, \
                             nc.sbuf_tensor(f"so0_{tag}{tau}", [128, 1024], F32) as so0, \
                             nc.sbuf_tensor(f"so1_{tag}{tau}", [128, 1024], F32) as so1, \
                             nc.psum_tensor(f"pd0_{tag}{tau}", [128, 1024], F32) as pd0, \
                             nc.psum_tensor(f"pd1_{tag}{tau}", [128, 1024], F32) as pd1:
                            wdh, xres, sx1 = [wdh0, wdh1, wdh2], [xr0, xr1], [so0, so1]
                            pd = [pd0, pd1]
                            halves = [(0, 22), (22, 43)]
                            dstt, stst, hload = {}, {}, {}
                            for mc in range(16):
                                for hi, (i0, i1) in enumerate(halves):
                                    hidx = mc * 2 + hi
                                    if hidx >= 3:
                                        wt(syn, "SY", "sPE", pe_done[("d", hidx - 3)])
                                    else:
                                        wt(syn, "SY", "sCC", cc_dn)
                                    hload[hidx] = ctx.lds(
                                        f"dh{hidx % 3}",
                                        wdh[hidx % 3][:][:, 0:(i1 - i0) * 128].rearrange(
                                            "p (i n) -> p i n", n=128),
                                        wdR[mc][:, i0:i1, :])
                                    wt(pe, "PE", f"dh{hidx % 3}", hload[hidx])
                                    if mc == 0 and hi == 0:
                                        wt(pe, "PE", "sDVE", hm_done[OC - 1])
                                    if mc >= 2 and hi == 0:
                                        wt(pe, "PE", "sDVE", dstt[mc - 2])
                                    for i in range(i0, i1):
                                        for hf in range(2):
                                            pe.matmul(
                                                pd[mc % 2][:, hf * 512:(hf + 1) * 512],
                                                wdh[hidx % 3][:][:, (i - i0) * 128:(i - i0 + 1) * 128],
                                                hbuf[:, i * 1024 + hf * 512:
                                                     i * 1024 + (hf + 1) * 512],
                                                start=(i == 0), stop=(i == OC - 1))
                                    pe_done[("d", hidx)] = ctx.dinc(pe, "sPE")
                                if mc >= 2:
                                    wt(syn, "SY", "sDVE", dstt[mc - 2])
                                vxr = ctx.lds(f"xr{mc % 2}", xres[mc % 2][:],
                                              res_r[mc][:, ts:ts + 1024])
                                wt(dve, "DVE", "sPE", pe_done[("d", mc * 2 + 1)])
                                wt(dve, "DVE", f"xr{mc % 2}", vxr)
                                if mc >= 2:
                                    wt(dve, "DVE", "qG", stst[mc - 2])
                                dve.scalar_tensor_tensor(sx1[mc % 2][:], pd[mc % 2][:], 1.0,
                                                         xres[mc % 2][:], ALU.mult, ALU.add)
                                dstt[mc] = ctx.dinc(dve, "sDVE")
                                wt(gp, "GP", "sDVE", dstt[mc])
                                stst[mc] = st(dst_r[mc][:, ts:ts + 1024], sx1[mc % 2][:])
                    ctx.fence()
            return cnt["qG"]

        x1_qG = emit_ffn(xTr, xTr, x1r, "f1", cc_gu=cc_u, cc_dn=cc_d, dep_qG=0)

        for tau in range(2):
            ts = tau * 1024
            with ExitStack() as pst2:
                p2e = pst2.enter_context
                xn2 = p2e(nc.sbuf_tensor(f"xn2_{tau}", [128, FC * 1024], BF16))
                wqk = [p2e(nc.sbuf_tensor(f"wqk{i}_{tau}", [128, 2048], BF16)) for i in range(3)]
                wqf = [p2e(nc.sbuf_tensor(f"wqf{i}_{tau}", [128, 2048], F32)) for i in range(3)]
                sqk = [p2e(nc.sbuf_tensor(f"sqk{i}_{tau}", [128, 1024], BF16)) for i in range(2)]
                embB = [p2e(nc.sbuf_tensor(f"emb{i}_{tau}", [128, 1024], F32)) for i in range(2)]
                v_xn2g = emit_ln(x1r, xn2, tau, f"q{tau}", dep_qG=x1_qG)
                v_xn2 = v_xn2g[1]
                if True:
                    qcv = {}

                    def qk_conv(j, qk_pe):
                        if j >= 32:
                            return
                        if j >= 3:
                            wt(syn, "SY", "sDVE", qcv[j - 3])
                        vlf = ctx.lds(f"qwf{j % 3}",
                                      wqf[j % 3][:].rearrange("p (kc n) -> p kc n", kc=16),
                                      qwfR[:, :, j * 128:(j + 1) * 128])
                        wt(dve, "DVE", f"qwf{j % 3}", vlf)
                        if j >= 3:
                            wt(dve, "DVE", "sPE", qk_pe.get(j - 3, 0))
                        dve.tensor_copy(wqk[j % 3][:], wqf[j % 3][:])
                        qcv[j] = ctx.dinc(dve, "sDVE")
                    v_eq = ctx.lds("emb", embB[0][:], embqI[:, ts:ts + 1024])
                    v_ek = ctx.lds("emb", embB[1][:], embkI[:, ts:ts + 1024])
                    with ExitStack() as pst3:
                        pqk = [pst3.enter_context(nc.psum_tensor(f"pqk{i}_{tau}", [128, 1024], F32)) for i in range(2)]
                        qk_pe, qk_dve, qk_st = {}, {}, {}
                        qk_conv(0, qk_pe)
                        qk_conv(1, qk_pe)
                        qk_conv(2, qk_pe)
                        for oc in range(32):
                            wt(pe, "PE", "sDVE", qcv[oc])
                            if oc >= 2:
                                wt(pe, "PE", "sDVE", qk_dve[oc - 2])
                            for hf in range(2):
                                wt(pe, "PE", "sDVE", v_xn2g[hf])
                                for kc in range(FC):
                                    pe.matmul(
                                        pqk[oc % 2][:, hf * 512:(hf + 1) * 512],
                                        wqk[oc % 3][:][:, kc * 128:(kc + 1) * 128],
                                        xn2[:, kc * 1024 + hf * 512: kc * 1024 + (hf + 1) * 512],
                                        start=(kc == 0), stop=(kc == FC - 1))
                            qk_pe[oc] = ctx.dinc(pe, "sPE")
                            qk_conv(oc + 3, qk_pe)
                            wt(dve, "DVE", "sPE", qk_pe[oc])
                            wt(dve, "DVE", "emb", max(v_eq, v_ek))
                            if oc >= 2:
                                wt(dve, "DVE", "qG", qk_st[oc - 2])
                            embt = embB[0][:] if oc < 16 else embB[1][:]
                            dve.scalar_tensor_tensor(sqk[oc % 2][:], pqk[oc % 2][:],
                                                     c_qkvbT[:, oc:oc + 1], embt,
                                                     ALU.add, ALU.mult)
                            qk_dve[oc] = ctx.dinc(dve, "sDVE")
                            wt(gp, "GP", "sDVE", qk_dve[oc])
                            h = oc % 16
                            row = (h // 2) * 512 + (0 if oc < 16 else 256) + (h % 2) * 128
                            qk_st[oc] = st(qk_in[tau].ap()[row:row + 128, 0:1024],
                                           sqk[oc % 2][:])
                    with ExitStack() as pst4:
                        p4e = pst4.enter_context
                        wv = p4e(nc.sbuf_tensor(f"wv_{tau}", [128, FC * 512], BF16))
                        wvf = p4e(nc.sbuf_tensor(f"wvf_{tau}", [128, FC * 512], F32))
                        svb = [p4e(nc.sbuf_tensor(f"sv{i}_{tau}", [128, 512], BF16)) for i in range(2)]
                        pv = [p4e(nc.psum_tensor(f"pv{i}_{tau}", [128, 512], F32)) for i in range(2)]
                        v_pe, v_dve, v_st = {}, {}, {}
                        vcv = 0
                        for ncc in range(4):
                            wt(syn, "SY", "sDVE", vcv)
                            vlvf = ctx.lds("wvf", wvf[:].rearrange("p (kc n) -> p kc n", kc=16),
                                           qwfR[:, :, 4096 + ncc * 512: 4096 + (ncc + 1) * 512])
                            wt(dve, "DVE", "wvf", vlvf)
                            if ncc >= 1:
                                wt(dve, "DVE", "sPE", v_pe[(ncc - 1, 7)])
                            dve.tensor_copy(wv[:], wvf[:])
                            vcv = ctx.dinc(dve, "sDVE")
                            for tc in range(8):
                                wt(pe, "PE", "sDVE", vcv)
                                bi = tc % 2
                                prev = None
                                if (ncc, tc) not in ((0, 0), (0, 1)):
                                    prev = (ncc, tc - 2) if tc >= 2 else (ncc - 1, 6 + tc)
                                if prev is not None:
                                    wt(pe, "PE", "sDVE", v_dve[prev])
                                for kc in range(FC):
                                    pe.matmul(
                                        pv[bi][:],
                                        xn2[:, kc * 1024 + tc * 128: kc * 1024 + (tc + 1) * 128],
                                        wv[:, kc * 512:(kc + 1) * 512],
                                        start=(kc == 0), stop=(kc == FC - 1))
                                v_pe[(ncc, tc)] = ctx.dinc(pe, "sPE")
                                wt(dve, "DVE", "sPE", v_pe[(ncc, tc)])
                                if prev is not None:
                                    wt(dve, "DVE", "qG", v_st[prev])
                                dve.scalar_tensor_tensor(
                                    svb[bi][:], pv[bi][:], 1.0,
                                    vbias_bc[:, ncc * 512:(ncc + 1) * 512],
                                    ALU.mult, ALU.add)
                                v_dve[(ncc, tc)] = ctx.dinc(dve, "sDVE")
                                wt(gp, "GP", "sDVE", v_dve[(ncc, tc)])
                                rbase = tc * 128
                                st(v_in[tau].ap()[(2 * ncc) * 1024 + rbase:
                                                  (2 * ncc) * 1024 + rbase + 128, :],
                                   svb[bi][:][:, 0:256])
                                v_st[(ncc, tc)] = st(
                                    v_in[tau].ap()[(2 * ncc + 1) * 1024 + rbase:
                                                   (2 * ncc + 1) * 1024 + rbase + 128, :],
                                    svb[bi][:][:, 256:512])
            wt(gp, "GP", "qG", cnt["qG"])
            with allow_ici_all_to_all():
                ctx.inc(gp.collective_compute("AllToAll", ALU.bypass,
                                              replica_groups=[list(range(R))],
                                              ins=[qk_in[tau].ap().opt()],
                                              outs=[qk_att[tau].ap().opt()]),
                        "sCC")
                cc_qkv = ctx.inc(
                    gp.collective_compute("AllToAll", ALU.bypass,
                                          replica_groups=[list(range(R))],
                                          ins=[v_in[tau].ap().opt()],
                                          outs=[v_att[tau].ap().opt()]),
                    "sCC")
            ctx.fence()

        p3stack = ExitStack()
        sow_all = p3stack.enter_context(nc.sbuf_tensor("b_swall", [128, FC * 2048], BF16))
        wof = [p3stack.enter_context(nc.sbuf_tensor(f"b_wf{i}", [128, 2048], F32))
               for i in range(2)]
        sow_state = {"done": 0}

        qkA = [t.ap() for t in qk_att]
        vA = [t.ap() for t in v_att]
        oI = [t.ap() for t in o_in]
        with ExitStack() as ast:
            ae = ast.enter_context
            sqt = [ae(nc.sbuf_tensor(f"a_sq{i}", [128, 512], BF16)) for i in range(2)]
            skb = [ae(nc.sbuf_tensor(f"a_sk{i}", [128, 512], BF16)) for i in range(4)]
            svk = [ae(nc.sbuf_tensor(f"a_sv{i}", [128, 4 * 128], BF16)) for i in range(4)]
            PTa = ae(nc.sbuf_tensor("a_pt", [128, 4096], BF16))
            psSa = ae(nc.psum_tensor("a_psS", [128, 2048], F32))
            poT = ae(nc.psum_tensor("a_poT", [128, 512], F32))
            psL = ae(nc.psum_tensor("a_psL", [1, 512], F32))
            lbc = ae(nc.psum_tensor("a_lbc", [128, 512], F32))
            poTs = ae(nc.sbuf_tensor("a_poTs", [128, 512], F32))
            lrow = ae(nc.sbuf_tensor("a_lrow", [1, 512], F32))
            linvb = ae(nc.sbuf_tensor("a_linv", [128, 512], F32))
            osb = ae(nc.sbuf_tensor("a_osb", [128, 512], BF16))

            exp_last = {0: 0, 1: 0}
            avq_last = {0: 0, 1: 0, 2: 0, 3: 0}
            s_hist, av_hist, q_hist = {}, {}, {}
            state = {"evac": 0, "norm": 0, "ost": 0}
            tail = [None]
            pend = []

            def post_av(g):
                kb_, p_, qt_, gk_, hp, aw, _tl = g
                if not (kb_ == qt_ and p_ == 1):
                    return
                eg = ctx.dinc(pe, "sPE")
                wt(dve, "DVE", "sPE", eg)
                dve.tensor_copy(poTs[:], poT[:])
                dve.tensor_copy(lrow[:], psL[:])
                state["evac"] = ctx.dinc(dve, "sDVE")
                tail[0] = _tl

            def emit_tail():
                if tail[0] is None:
                    return
                rq_, hl_, qbase_ = tail[0]
                wt(pe, "PE", "sDVE", state["evac"])
                wt(pe, "PE", "sDVE", state["norm"])
                pe.matmul(lbc[:], ones1[:], lrow[:], start=True, stop=True)
                vB = ctx.dinc(pe, "sPE")
                wt(dve, "DVE", "sPE", vB)
                wt(dve, "DVE", "qG", state["ost"])
                dve.reciprocal(linvb[:], lbc[:])
                dve.tensor_tensor(osb[:], poTs[:], linvb[:], ALU.mult)
                state["norm"] = ctx.dinc(dve, "sDVE")
                wt(gp, "GP", "sDVE", state["norm"])
                orow = rq_ * 128
                state["ost"] = st(oI[hl_][orow:orow + 128, qbase_:qbase_ + 512], osb[:])
                tail[0] = None

            def emit_av(g):
                kb_, p_, qt_, gk_, hp, aw, _tl = g
                if kb_ == 0 and p_ == 0:
                    emit_tail()
                    wt(pe, "PE", "sDVE", state["evac"])
                wt(pe, "PE", aw["vs"], aw["vv"])
                wt(pe, "PE", aw["w"][0], aw["w"][1])
                svr_ = svk[gk_ % 4][:].rearrange("p (s n) -> p s n", s=4)
                pcol = hp * 1024
                mm = None
                for si in range(2):
                    s = 2 * p_ + si
                    pe.matmul(psL[:], onesb[:],
                              PTa[:, pcol + si * 512:pcol + (si + 1) * 512],
                              start=(kb_ == 0 and s == 0),
                              stop=(kb_ == qt_ and s == 3))
                for si in range(2):
                    s = 2 * p_ + si
                    mm = pe.matmul(poT[:], svr_[:, s],
                                   PTa[:, pcol + si * 512:pcol + (si + 1) * 512],
                                   start=(kb_ == 0 and s == 0),
                                   stop=(kb_ == qt_ and s == 3))
                avv = ctx.inc(mm, "sPE")
                avq_last[hp] = avv
                av_hist[gk_] = avv
                return avv

            it = 0
            gk = 0
            gq = 0
            q_last_S = {}
            cc_oh = {}
            for hl in range(2):
                for b in range(2):
                    for qt in range(16):
                        jq, qbase = key_loc(qt)
                        rq = 4 * b + jq
                        wt(syn, "SY", "sCC", cc_qkv)
                        if it >= 2:
                            wt(syn, "SY", "sPE", q_last_S.get(it - 2, 0))
                        q_hist[it] = ctx.lds(
                            f"sq{it % 2}", sqt[it % 2][:],
                            qkA[qbase // 1024][rq * 512 + hl * 128:
                                               rq * 512 + hl * 128 + 128,
                                qbase % 1024:qbase % 1024 + 512])
                        for kb in range(qt + 1):
                            jk, kbase = key_loc(kb)
                            rk = 4 * b + jk
                            krow = rk * 512 + 256 + hl * 128
                            ktau, kloc = kbase // 1024, kbase % 1024
                            vrow = rk * 1024 + kloc
                            if gk >= 4:
                                wt(syn, "SY", "sPE", s_hist.get(gk - 4, 0))
                                wt(syn, "SY", "sPE", av_hist.get(gk - 4, 0))
                            ctx.lds(f"kv{gk % 4}", skb[gk % 4][:],
                                    qkA[ktau][krow:krow + 128, kloc:kloc + 512])
                            vv = ctx.lds(f"kv{gk % 4}",
                                         svk[gk % 4][:].rearrange("p (s n) -> p s n", s=4),
                                         vA[ktau][vrow:vrow + 512, hl * 128:(hl + 1) * 128]
                                         .rearrange("(s p) n -> p s n", s=4, p=128))
                            wt(pe, "PE", f"kv{gk % 4}", vv)
                            wt(pe, "PE", f"sq{it % 2}", q_hist[it])
                            for p in range(2):
                                wt(pe, "PE", "sACT", exp_last[p])
                                for si in range(2):
                                    s = 2 * p + si
                                    pe.matmul(psSa[:, s * 512:(s + 1) * 512],
                                              skb[gk % 4][:][:, s * 128:(s + 1) * 128],
                                              sqt[it % 2][:], start=True, stop=True)
                            vS = ctx.dinc(pe, "sPE")
                            s_hist[gk] = vS
                            q_last_S[it] = vS
                            for p in range(2):
                                hp = gq % 4
                                gq += 1
                                pcol = hp * 1024
                                wt(act, "ACT", "sPE", vS)
                                wt(act, "ACT", "sPE", avq_last[hp])
                                act.activation(PTa[:, pcol:pcol + 1024],
                                               psSa[:, p * 1024:(p + 1) * 1024],
                                               AF.Exp, bias=shiftc[:], scale=1.0)
                                exp_last[p] = ctx.dinc(act, "sACT")
                                aw = {"vv": vv, "vs": f"kv{gk % 4}",
                                      "w": ("sACT", exp_last[p])}
                                if kb == qt:
                                    wt(dve, "DVE", "sACT", exp_last[p])
                                    dve.tensor_tensor(
                                        PTa[:, pcol:pcol + 1024],
                                        PTa[:, pcol:pcol + 1024],
                                        c_masks[:, p * 1024:(p + 1) * 1024], ALU.mult)
                                    aw = {"vv": vv, "vs": f"kv{gk % 4}",
                                          "w": ("sDVE", ctx.dinc(dve, "sDVE"))}
                                pend.append((kb, p, qt, gk, hp, aw, (rq, hl, qbase)))
                                if len(pend) > 2:
                                    g = pend.pop(0)
                                    emit_av(g)
                                    post_av(g)
                            gk += 1
                        it += 1
                if hl == 0:
                    while pend:
                        g = pend.pop(0)
                        emit_av(g)
                        post_av(g)
                    emit_tail()
                    socv = {}
                    for hd in range(16):
                        if hd >= 2:
                            wt(syn, "SY", "sDVE", socv[hd - 2])
                        vf = ctx.lds(f"wof{hd % 2}", wof[hd % 2][:], owfr[hd])
                        wt(dve, "DVE", f"wof{hd % 2}", vf)
                        dve.tensor_copy(sow_all[:, hd * 2048:(hd + 1) * 2048],
                                        wof[hd % 2][:])
                        socv[hd] = ctx.dinc(dve, "sDVE")
                    sow_state["done"] = socv[15]
                    wt(gp, "GP", "qG", cnt["qG"])
                    with allow_ici_all_to_all():
                        cc_oh[0] = ctx.inc(
                            gp.collective_compute("AllToAll", ALU.bypass,
                                                  replica_groups=[list(range(R))],
                                                  ins=[o_in[0].ap().opt()],
                                                  outs=[o_att[0].ap().opt()]),
                            "sCC")
            while pend:
                g = pend.pop(0)
                emit_av(g)
                post_av(g)
            emit_tail()

        wt(gp, "GP", "qG", cnt["qG"])
        with allow_ici_all_to_all():
            cc_o = ctx.inc(
                gp.collective_compute("AllToAll", ALU.bypass,
                                      replica_groups=[list(range(R))],
                                      ins=[o_in[1].ap().opt()], outs=[o_att[1].ap().opt()]),
                "sCC")

        oA = [t.ap() for t in o_att]
        att_pe_final = cnt["sPE"]
        att_dve_final = cnt["sDVE"]
        sow_done = sow_state["done"]
        cc_o0 = cc_oh[0]
        with ExitStack() as bst:
            be = bst.enter_context
            ofm = [be(nc.sbuf_tensor(f"b_of{i}", [128, FC * 512], BF16)) for i in range(2)]
            x1c = [be(nc.sbuf_tensor(f"b_x1{i}", [128, 512], F32)) for i in range(2)]
            x2c = [be(nc.sbuf_tensor(f"b_x2{i}", [128, 512], F32)) for i in range(2)]
            wt(syn, "SY", "sPE", att_pe_final)
            wt(syn, "SY", "sDVE", att_dve_final)
            with nc.psum_tensor("b_j0", [128, 512], F32) as bj0, \
                 nc.psum_tensor("b_j1", [128, 512], F32) as bj1:
                pj = [bj0, bj1]
                op_pe, op_dve, op_st = {}, {}, {}
                qb_last_pe, vofm = {}, {}
                for qb in range(4):
                    q0 = qb * 512
                    if qb >= 2:
                        wt(syn, "SY", "sPE", qb_last_pe[qb - 2])
                    wt(syn, "SY", "sCC", cc_o0)
                    for fc in range(0, 16, 2):
                        ctx.lds(f"of{qb % 2}",
                                ofm[qb % 2][:, fc * 512:(fc + 1) * 512],
                                oA[0][(fc // 2) * 128:(fc // 2) * 128 + 128,
                                      q0:q0 + 512])
                    wt(syn, "SY", "sCC", cc_o)
                    v = 0
                    for fc in range(1, 16, 2):
                        v = ctx.lds(f"of{qb % 2}",
                                    ofm[qb % 2][:, fc * 512:(fc + 1) * 512],
                                    oA[1][(fc // 2) * 128:(fc // 2) * 128 + 128,
                                          q0:q0 + 512])
                    vofm[qb] = v
                    for mc in range(16):
                        gi = qb * 16 + mc
                        if gi >= 2:
                            wt(syn, "SY", "sPE", op_pe[gi - 2])
                            wt(syn, "SY", "sDVE", op_dve[gi - 2])
                        vx1 = ctx.lds(f"so{gi % 2}", x1c[gi % 2][:],
                                      x1r[mc][:, q0:q0 + 512])
                        wt(pe, "PE", "sDVE", sow_done)
                        wt(pe, "PE", f"of{qb % 2}", vofm[qb])
                        if gi == 0:
                            wt(pe, "PE", "sPE", att_pe_final)
                            wt(pe, "PE", "sDVE", att_dve_final)
                        if gi >= 2:
                            wt(pe, "PE", "sDVE", op_dve[gi - 2])
                        for hd in range(16):
                            pe.matmul(pj[gi % 2][:],
                                      sow_all[:, hd * 2048 + mc * 128:
                                              hd * 2048 + (mc + 1) * 128],
                                      ofm[qb % 2][:, hd * 512:(hd + 1) * 512],
                                      start=(hd == 0), stop=(hd == 15))
                        op_pe[gi] = ctx.dinc(pe, "sPE")
                        wt(dve, "DVE", "sPE", op_pe[gi])
                        wt(dve, "DVE", f"so{gi % 2}", vx1)
                        if gi >= 2:
                            wt(dve, "DVE", "qG", op_st[gi - 2])
                        dve.scalar_tensor_tensor(x2c[gi % 2][:], pj[gi % 2][:],
                                                 c_outbT[:, mc:mc + 1], x1c[gi % 2][:],
                                                 ALU.add, ALU.add)
                        op_dve[gi] = ctx.dinc(dve, "sDVE")
                        wt(gp, "GP", "sDVE", op_dve[gi])
                        op_st[gi] = st(x2r[mc][:, q0:q0 + 512], x2c[gi % 2][:])
                    qb_last_pe[qb] = op_pe[qb * 16 + 15]
        p3stack.close()
        x2_qG = cnt["qG"]
        ctx.fence()

        emit_ffn(x2r, x2r, outr, "f2", cc_gu=cc_u, cc_dn=cc_d, dep_qG=x2_qG)

        if debug:
            dbg_pairs = [("dx1", x1D), ("dx2", x2D), ("dwg", wgD)]
            ctx.fence()
            for nm, ten in dbg_pairs:
                dd = dt_(nm, list(ten.shape), ten.dtype, kind="ExternalOutput")
                st(dd.ap(), ten.ap())

        wt(gp, "GP", "qG", cnt["qG"])
        wt(syn, "SY", "qG", cnt["qG"])
        wt(syn, "SY", "qS", cnt["qS"])
    return nc


def _host_prepare(inputs):
    x = np.asarray(inputs["L0_attn_x"], dtype=np.float32)
    qkv_w = np.asarray(inputs["L0_attn_qkv_w"], dtype=np.float32)
    qkv_b = np.asarray(inputs["L0_attn_qkv_b"], dtype=np.float32)
    out_w = np.asarray(inputs["L0_attn_out_w"], dtype=np.float32)
    out_b = np.asarray(inputs["L0_attn_out_b"], dtype=np.float32)
    lut = np.asarray(inputs["L0_ffn_lut"], dtype=np.float32)
    g_idx = np.asarray(inputs["L0_ffn_gate_idx"], dtype=np.int32)
    u_idx = np.asarray(inputs["L0_ffn_up_idx"], dtype=np.int32)
    d_idx = np.asarray(inputs["L0_ffn_down_idx"], dtype=np.int32)

    qkvwT = np.ascontiguousarray(qkv_w.T)
    qkvbT = np.ascontiguousarray(qkv_b.reshape(48, 128).T)
    vbias = np.ascontiguousarray(qkv_b[4096:].reshape(1, D))
    outwT = np.ascontiguousarray(out_w.T)
    outbT = np.ascontiguousarray(out_b.reshape(16, 128).T)
    lut2 = np.ascontiguousarray(lut.reshape(1, LUT))
    gT = np.ascontiguousarray(g_idx.T)
    uT = np.ascontiguousarray(u_idx.T)
    dT = np.ascontiguousarray(d_idx.T)

    freqs = np.arange(0, HD, 2, dtype=np.float32) / HD
    pos = np.arange(SEQ, dtype=np.float32)
    ang = pos[:, None] * freqs[None, :]
    emb = np.concatenate([np.sin(ang), np.cos(ang)], axis=-1)
    embT = np.ascontiguousarray(emb.T.astype(np.float32))

    kk = np.arange(128)[:, None]
    qq = np.arange(512)[None, :]
    masks = np.concatenate(
        [(128 * m + kk <= qq) for m in range(4)], axis=1).astype(ml_dtypes.bfloat16)

    in_maps, metas = [], []
    for cid in range(R):
        bb = cid // 4
        jj = cid % 4
        tok = np.concatenate([np.arange(ca * 1024, (ca + 1) * 1024) for ca in (jj, 7 - jj)])
        im = {
            "xT": np.ascontiguousarray(x[bb][tok].T),
            "qkvwT": qkvwT, "qkvbT": qkvbT, "vbias": vbias,
            "outwT": outwT, "outbT": outbT, "lut": lut2,
            "gidx": np.ascontiguousarray(gT[cid * 256:(cid + 1) * 256]),
            "uidx": np.ascontiguousarray(uT[cid * 256:(cid + 1) * 256]),
            "didx": np.ascontiguousarray(dT[:, cid * 256:(cid + 1) * 256]),
            "embq": np.ascontiguousarray(embT[:, tok] * np.float32(1.0 / np.sqrt(HD))),
            "embk": np.ascontiguousarray(embT[:, tok]),
            "masks": masks,
        }
        in_maps.append(im)
        metas.append((bb, tok))
    return in_maps, metas


def kernel(**inputs):
    in_maps, metas = _host_prepare(inputs)
    nc = build()
    res = run_bass_kernel_spmd(nc, in_maps, core_ids=list(range(R)))
    out = np.empty((B, SEQ, D), dtype=np.float32)
    for cid in range(R):
        bb, tok = metas[cid]
        out[bb, tok, :] = np.asarray(res.results[cid]["out"]).T
    return out
